# version 03
# baseline (speedup 1.0000x reference)
import numpy as np
import sys
from contextlib import ExitStack
sys.path.insert(0, '/opt/trn_rl_repo')
import concourse.bass as bass
import concourse.mybir as mybir
from concourse.bass_utils import run_bass_kernel_spmd
from concourse.ap import AP as RawAP
import ml_dtypes

F32 = mybir.dt.float32
BF16 = mybir.dt.bfloat16
U8 = mybir.dt.uint8
AF = mybir.ActivationFunctionType
ALU = mybir.AluOpType
AX = mybir.AxisListType
BF = ml_dtypes.bfloat16

D = 1024
NH = 16
DH = 64
S = 4096
B = 4
NCORE = 8
ROWS = 2048


class Ch:
    ENGS = ("sync", "gp", "v", "s", "t")

    def __init__(self, nc):
        self.nc = nc
        self.sem = {}
        self.cnt = {e: 0 for e in self.ENGS}
        self.last = None
        self.prev_small = {e: False for e in self.ENGS}
        self.q = {e: [] for e in self.ENGS}

    def open_sems(self, stack):
        for e in self.ENGS:
            self.sem[e] = stack.enter_context(self.nc.semaphore("chs_" + e))

    def tok(self, e=None):
        if e is None:
            e = self.last[0]
        return (e, self.cnt[e])

    def run(self, e, fn, wait=None, small=False, dma=False, fence=False, chain=True):
        waits = []
        if wait is not None:
            waits = [w for w in wait if w[1] > 0]
        elif chain and self.last is not None:
            le, lc = self.last
            if le != e:
                waits = [(le, lc)]
            elif fence or self.prev_small[e]:
                waits = [(e, lc)]
        elif fence and self.last is not None and self.last[0] == e:
            waits = [(e, self.last[1])]
        step = 16 if dma else 1
        self.q[e].append((waits, fn, step))
        self.cnt[e] += step
        t = (e, self.cnt[e])
        if chain:
            self.last = t
        self.prev_small[e] = small
        return t

    def flush(self, block):
        def replay(ename):
            def f(eng):
                sem = self.sem
                for (waits, fn, step) in self.q[ename]:
                    for (we, wc) in waits:
                        eng.wait_ge(sem[we], wc)
                    instr = fn(eng)
                    instr.then_inc(sem[ename], step)
            return f
        block.sync(replay("sync"))
        block.gpsimd(replay("gp"))
        block.vector(replay("v"))
        block.scalar(replay("s"))
        block.tensor(replay("t"))


def make_consts():
    c = {}
    c["ident"] = np.eye(128, dtype=BF)
    c["identf"] = np.eye(128, dtype=np.float32)
    big = np.float32(1e30)
    q = np.arange(128)[:, None]
    k = np.arange(512)[None, :]
    for s_ in range(4):
        c[f"triq_{s_}"] = np.where(k > q + 128 * s_, -big, np.float32(0)).astype(np.float32)
    ki = np.arange(128)[:, None]
    qj = np.arange(512)[None, :]
    for d_ in range(4):
        c[f"trik_{d_}"] = (ki + 128 * d_ <= qj).astype(BF)
        c[f"trikq_{d_}"] = np.where(ki + 128 * d_ > qj, -big, np.float32(0)).astype(np.float32)
    c["poskq"] = np.broadcast_to(np.arange(512, dtype=np.float32)[None, :], (128, 512)).copy()
    c["poskk"] = (np.arange(128)[:, None] + 128 * np.arange(32)[None, :]).astype(np.float32)
    inv = 1.0 / (10000.0 ** (np.arange(0, DH, 2, dtype=np.float32) / DH))
    pos = np.arange(S, dtype=np.float32)
    fr = pos[:, None] * inv[None, :]
    c["ropec"] = np.cos(fr).astype(np.float32)
    c["ropes"] = np.sin(fr).astype(np.float32)
    return c



EPS = 1e-5
MYH = 8
MYF = MYH * DH
LQW = 2304
LQT = 2431
TMAX = 1151
W2K = 2172
NITER_TOPK = 14


class NS(dict):
    __getattr__ = dict.__getitem__
    __setattr__ = dict.__setitem__



def ln_tile(ch, n, x_ap, xn_ap):
    sm, dump = n.sm, n.dump
    ch.run("v", lambda e: e.reduce_sum(out=sm[:, 0:1], in_=x_ap, axis=AX.X), small=True)
    ch.run("s", lambda e: e.activation(out=dump[:], in_=x_ap, func=AF.Square,
                                       bias=0.0, scale=1.0, accum_out=sm[:, 1:2]), small=True)
    ch.run("v", lambda e: e.tensor_scalar(out=sm[:, 2:3], in0=sm[:, 0:1], scalar1=1.0 / D,
                                          scalar2=None, op0=ALU.mult), small=True)
    ch.run("v", lambda e: e.tensor_scalar(out=sm[:, 3:4], in0=sm[:, 1:2], scalar1=1.0 / D,
                                          scalar2=None, op0=ALU.mult), small=True)
    ch.run("v", lambda e: e.tensor_tensor(out=sm[:, 4:5], in0=sm[:, 2:3], in1=sm[:, 2:3],
                                          op=ALU.mult), small=True)
    ch.run("v", lambda e: e.tensor_tensor(out=sm[:, 5:6], in0=sm[:, 3:4], in1=sm[:, 4:5],
                                          op=ALU.subtract), small=True)
    ch.run("s", lambda e: e.activation(out=sm[:, 6:7], in_=sm[:, 5:6], func=AF.Sqrt,
                                       bias=n.epsv[:], scale=1.0), small=True)
    ch.run("v", lambda e: e.reciprocal(out=sm[:, 7:8], in_=sm[:, 6:7]), small=True)
    ch.run("v", lambda e: e.tensor_scalar(out=xn_ap, in0=x_ap, scalar1=sm[:, 2:3],
                                          scalar2=sm[:, 7:8], op0=ALU.subtract, op1=ALU.mult))


def transpose_128(ch, n, out_ap_fn, in_ap_fn, nch):
    for k in range(nch):
        ch.run("t", lambda e, k=k: e.transpose(out=n.pt_bf[:, 128 * k:128 * (k + 1)],
                                               in_=in_ap_fn(k), identity=n.ident[:]))
    for k in range(nch):
        ch.run("v", lambda e, k=k: e.tensor_copy(out=out_ap_fn(k),
                                                 in_=n.pt_bf[:, 128 * k:128 * (k + 1)]))


def load_wT(ch, n, src_ap, wT, rows, cols=1024, sign=False):
    for j in range(rows // 128):
        ch.run("sync", lambda e, j=j, cols=cols: e.dma_start(out=n.wtmp_f[:, 0:cols], in_=src_ap[128 * j:128 * (j + 1), :]), dma=True)
        if sign:
            ch.run("s", lambda e, cols=cols: e.sign(out=n.wtmp_b[:, 0:cols], in_=n.wtmp_f[:, 0:cols]))
        else:
            ch.run("v", lambda e, cols=cols: e.tensor_copy(out=n.wtmp_b[:, 0:cols], in_=n.wtmp_f[:, 0:cols]))
        transpose_128(ch, n, lambda k, j=j: wT[k][:, 128 * j:128 * (j + 1)],
                      lambda k: n.wtmp_b[:, 128 * k:128 * (k + 1)], cols // 128)


def bcast_row(ch, n, out_sb, row_ap, ncols):
    for j in range(0, ncols, 512):
        w = min(512, ncols - j)
        ch.run("t", lambda e, j=j, w=w: e.matmul(out=n.pbc[:, 0:w], lhsT=n.onescol_f[0:1, :],
                                                 rhs=row_ap[0:1, j:j + w], start=True, stop=True))
        ch.run("v", lambda e, j=j, w=w: e.tensor_copy(out=out_sb[:, j:j + w], in_=n.pbc[:, 0:w]), small=(w < 256))



def build_attn(cfg):
    nc = bass.Bass("TRN2", target_bir_lowering=False)
    ap = {}
    def din(name, shape, dt=F32):
        ap[name] = nc.dram_tensor(name, shape, dt, kind="ExternalInput").ap()
    din("x", [ROWS, D])
    din("wqkv", [3 * MYF, D])
    din("bqkv", [1, 3 * MYF])
    din("wo", [D, MYF])
    din("bo", [1, D])
    din("ident", [128, 128], BF16)
    if cfg["causal"]:
        for s_ in range(4):
            din(f"triq_{s_}", [128, 512])
            din(f"trik_{s_}", [128, 512], BF16)
    if cfg["decay"]:
        din("poskq", [128, 512])
        din("poskk", [128, 32])
        for d_ in range(4):
            din(f"trikq_{d_}", [128, 512])
    if cfg["rope"]:
        din("ropec", [S, DH // 2])
        din("ropes", [S, DH // 2])
    if cfg["topk"]:
        din("identf", [128, 128])
    if cfg["relbias"]:
        din("bexr", [MYH, LQT])
        din("bex", [MYH, LQT])
        din("bfar", [1, 2 * MYH])
    if cfg["gates"]:
        din("gates", [1, MYH])
    xo = nc.dram_tensor("xo", [ROWS, D], F32, kind="ExternalOutput").ap()

    st = ExitStack()
    with st:
        ch = Ch(nc)
        ch.open_sems(st)
        attn_stage(nc, ch, st, cfg, ap, xo, "")
        with nc.Block() as block:
            ch.flush(block)
    return nc


def attn_stage(nc, ch, stage_st, cfg, ap, xo, pfx):
    dr = NS()
    dr.xn_d = nc.dram_tensor(pfx + "xn_d", [ROWS, D], BF16).ap()
    dr.xg_d = nc.dram_tensor(pfx + "xg_d", [S, D], BF16).ap()
    dr.qf_d = nc.dram_tensor(pfx + "qf_d", [MYF, S], BF16).ap()
    dr.kf_d = nc.dram_tensor(pfx + "kf_d", [MYF, S], BF16).ap()
    dr.vt_d = nc.dram_tensor(pfx + "vt_d", [S, MYF], BF16).ap()
    if cfg["topk"]:
        dr.qflo_d = nc.dram_tensor(pfx + "qflo_d", [MYF, S], BF16).ap()
        dr.kflo_d = nc.dram_tensor(pfx + "kflo_d", [MYF, S], BF16).ap()
        dr.xnlo_d = nc.dram_tensor(pfx + "xnlo_d", [ROWS, D], BF16).ap()
        dr.xglo_d = nc.dram_tensor(pfx + "xglo_d", [S, D], BF16).ap()
    dr.y_d = nc.dram_tensor(pfx + "y_d", [S, D], F32).ap()
    dr.yr_d = nc.dram_tensor(pfx + "yr_d", [ROWS, D], F32).ap()
    st = stage_st
    if True:
        n = NS()
        def sb(name, shape, dt):
            t = st.enter_context(nc.sbuf_tensor(pfx + "sb_" + name, shape, dt))
            n[name] = t
            return t
        sb("ident", [128, 128], BF16)
        n.wqkvT = [st.enter_context(nc.sbuf_tensor(pfx + f"wqkvT{k}", [128, 3 * MYF], BF16)) for k in range(8)]
        n.woT = [st.enter_context(nc.sbuf_tensor(pfx + f"woT{k}", [128, D], BF16)) for k in range(4)]
        sb("wtmp_f", [128, D], F32); sb("wtmp_b", [128, D], BF16)
        sb("x_t", [128, D], F32); sb("xb_t", [128, D], F32)
        sb("dump", [128, D], F32); sb("sm", [128, 16], F32); sb("epsv", [128, 1], F32)
        sb("xn_t", [128, D], BF16); sb("xn_f", [128, D], BF16)
        sb("qkv_t", [128, 3 * MYF], F32); sb("qkv_b", [128, 2 * MYF], BF16)
        sb("bqkv_bc", [128, 3 * MYF], F32); sb("bo_bc", [128, D], F32)
        sb("bqkv_r", [1, 3 * MYF], F32); sb("bo_r", [1, D], F32)
        sb("onescol_f", [1, 128], F32); sb("onescol_b", [128, 1], BF16)
        sb("onesrow_b", [1, 512], BF16)
        if cfg["causal"]:
            n.triq = [st.enter_context(nc.sbuf_tensor(pfx + f"triqs{i}", [128, 512], F32)) for i in range(4)]
            n.trik = [st.enter_context(nc.sbuf_tensor(pfx + f"triks{i}", [128, 512], BF16)) for i in range(4)]
        if cfg["rope"]:
            sb("rc_t", [128, DH // 2], F32); sb("rs_t", [128, DH // 2], F32)
            sb("rtmp", [128, MYF], F32)
        if cfg["decay"]:
            sb("poskq", [128, 512], F32); sb("poskk", [128, 32], F32)
            sb("kcoff", [128, 8], F32)
            n.trikq = [st.enter_context(nc.sbuf_tensor(pfx + f"trikqs{i}", [128, 512], F32)) for i in range(4)]
        sb("kh", [66, S], BF16); sb("qa", [66, 512], BF16)
        sb("stmp", [128, 512], F32); sb("cm", [128, 16], F32)
        sb("Mv", [128, 1], F32); sb("mrow_b", [128, 1], BF16)
        sb("PT", [128, 512], BF16); sb("PT2", [128, 512], BF16)
        sb("vc", [128, MYF], BF16)
        sb("osb", [128, 32 * MYF], BF16)
        sb("rz", [128, 4], F32)
        sb("off_t", [128, D], BF16)
        if cfg["topk"]:
            sb("khlo", [66, S], BF16); sb("qalo", [66, 512], BF16)
            sb("qkvlo_b", [128, 2 * MYF], BF16)
            sb("identf", [128, 128], F32)
            sb("xn32", [128, D], F32)
            sb("xnlo_t", [128, D], BF16); sb("xnlo_f", [128, D], BF16)
            sb("Mb32", [128, 1], F32)
            sb("taur_f", [1, 512], F32)
            n.wqkvTlo = [st.enter_context(nc.sbuf_tensor(pfx + f"wqkvTlo{k}", [128, 2 * MYF], BF16)) for k in range(8)]
            sb("ssb", [128, S], F32); sb("segm", [128, 64], F32)
            sb("lov", [128, 1], F32); sb("hiv", [128, 1], F32)
            sb("midv", [128, 1], F32); sb("cntv", [128, 1], F32)
            sb("mskv", [128, 1], U8); sb("tjunk", [128, S], BF16)
            sb("taub", [128, 512], F32)
        if cfg["relbias"]:
            sb("skq", [128, LQW], BF16); sb("skk", [128, W2K], BF16)
            sb("bfar_r", [1, 2 * MYH], F32)
            sb("farv", [128, 2], F32)
        if cfg["gates"]:
            sb("gate_r", [1, MYH], F32); sb("gate_bc", [128, MYH], F32)
        n.pb = [st.enter_context(nc.psum_tensor(pfx + f"pb{i}", [128, 512], F32)) for i in range(6)]
        n.pbc = st.enter_context(nc.psum_tensor(pfx + "pbc", [128, 512], F32))
        n.pt_bf = st.enter_context(nc.psum_tensor(pfx + "pt_bf", [128, 1024], BF16))
        n.ap = ap
        n.dr = dr
        n.xo = xo
        emit_attn(ch, cfg, n)


def emit_attn(ch, cfg, n):
    ap, dr = n.ap, n.dr
    pb = n.pb

    def coll(kind, op, in_ap, out_ap):
        ch.run("gp", lambda e: e.collective_compute(
            kind, op, replica_groups=[[0, 1], [2, 3], [4, 5], [6, 7]],
            ins=[in_ap.opt()], outs=[out_ap.opt()]))

    ch.run("v", lambda e: e.memset(n.onescol_f[:], 1.0))
    ch.run("v", lambda e: e.memset(n.epsv[:], float(EPS)), small=True)
    ch.run("v", lambda e: e.memset(n.onescol_b[:], 1.0))
    ch.run("v", lambda e: e.memset(n.onesrow_b[:], 1.0))
    ch.run("v", lambda e: e.memset(n.kh[64:66, :], 0.0))
    ch.run("v", lambda e: e.memset(n.kh[64:65, :], 1.0), fence=True)
    ch.run("v", lambda e: e.memset(n.qa[64:66, :], 0.0))
    if cfg["topk"]:
        ch.run("v", lambda e: e.memset(n.khlo[64:66, :], 0.0))
        ch.run("v", lambda e: e.memset(n.qalo[64:66, :], 0.0))
    ch.run("sync", lambda e: e.dma_start(out=n.ident[:], in_=ap["ident"]), dma=True)
    if cfg["causal"]:
        for s_ in range(4):
            ch.run("sync", lambda e, s_=s_: e.dma_start(out=n.triq[s_][:], in_=ap[f"triq_{s_}"]), dma=True)
            ch.run("sync", lambda e, s_=s_: e.dma_start(out=n.trik[s_][:], in_=ap[f"trik_{s_}"]), dma=True)
    ch.run("sync", lambda e: e.dma_start(out=n.bqkv_r[:], in_=ap["bqkv"]), dma=True)
    ch.run("sync", lambda e: e.dma_start(out=n.bo_r[:], in_=ap["bo"]), dma=True)
    if cfg["decay"]:
        ch.run("sync", lambda e: e.dma_start(out=n.poskq[:], in_=ap["poskq"]), dma=True)
        ch.run("sync", lambda e: e.dma_start(out=n.poskk[:], in_=ap["poskk"]), dma=True)
        for d_ in range(4):
            ch.run("sync", lambda e, d_=d_: e.dma_start(out=n.trikq[d_][:], in_=ap[f"trikq_{d_}"]), dma=True)
        for c in range(8):
            ch.run("v", lambda e, c=c: e.memset(n.kcoff[:, c:c + 1], float(512 * c)), small=True)
    if cfg["relbias"]:
        ch.run("sync", lambda e: e.dma_start(out=n.bfar_r[:], in_=ap["bfar"]), dma=True)
    if cfg["gates"]:
        ch.run("sync", lambda e: e.dma_start(out=n.gate_r[:], in_=ap["gates"]), dma=True)
        bcast_row(ch, n, n.gate_bc, n.gate_r[:], MYH)
    bcast_row(ch, n, n.bqkv_bc, n.bqkv_r[:], 3 * MYF)
    bcast_row(ch, n, n.bo_bc, n.bo_r[:], D)
    qsc = cfg["qscale"]
    if qsc != 1.0:
        ch.run("v", lambda e: e.tensor_scalar(out=n.bqkv_bc[:, 0:MYF], in0=n.bqkv_bc[:, 0:MYF],
                                              scalar1=qsc, scalar2=None, op0=ALU.mult), fence=True)
    if cfg["topk"]:
        ch.run("sync", lambda e: e.dma_start(out=n.identf[:], in_=ap["identf"]), dma=True)
    load_wT(ch, n, ap["wqkv"], n.wqkvT, 3 * MYF)
    if cfg["topk"]:
        for j in range(2 * MYF // 128):
            ch.run("sync", lambda e, j=j: e.dma_start(out=n.wtmp_f[:], in_=ap["wqkv"][128 * j:128 * (j + 1), :]), dma=True)
            ch.run("v", lambda e: e.tensor_copy(out=n.wtmp_b[:], in_=n.wtmp_f[:]))
            ch.run("v", lambda e: e.tensor_tensor(out=n.dump[:], in0=n.wtmp_f[:], in1=n.wtmp_b[:], op=ALU.subtract))
            ch.run("v", lambda e: e.tensor_copy(out=n.wtmp_b[:], in_=n.dump[:]), fence=True)
            transpose_128(ch, n, lambda k, j=j: n.wqkvTlo[k][:, 128 * j:128 * (j + 1)],
                          lambda k: n.wtmp_b[:, 128 * k:128 * (k + 1)], 8)
    load_wT(ch, n, ap["wo"], n.woT, D, cols=MYF)

    for t in range(ROWS // 128):
        ch.run("sync", lambda e, t=t: e.dma_start(out=n.x_t[:], in_=ap["x"][128 * t:128 * (t + 1), :]), dma=True)
        if cfg["topk"]:
            ln_tile(ch, n, n.x_t[:], n.xn32[:])
            ch.run("v", lambda e: e.tensor_copy(out=n.xn_t[:], in_=n.xn32[:]))
            ch.run("v", lambda e: e.tensor_tensor(out=n.dump[:], in0=n.xn32[:], in1=n.xn_t[:], op=ALU.subtract))
            ch.run("v", lambda e: e.tensor_copy(out=n.xnlo_t[:], in_=n.dump[:]))
            ch.run("sync", lambda e, t=t: e.dma_start(out=dr.xnlo_d[128 * t:128 * (t + 1), :], in_=n.xnlo_t[:]), dma=True)
        else:
            ln_tile(ch, n, n.x_t[:], n.xn_t[:])
        ch.run("sync", lambda e, t=t: e.dma_start(out=dr.xn_d[128 * t:128 * (t + 1), :], in_=n.xn_t[:]), dma=True)

    coll("AllGather", ALU.bypass, dr.xn_d, dr.xg_d)
    if cfg["topk"]:
        coll("AllGather", ALU.bypass, dr.xnlo_d, dr.xglo_d)

    for t in range(S // 128):
        ch.run("sync", lambda e, t=t: e.dma_start(out=n.xn_t[:], in_=dr.xg_d[128 * t:128 * (t + 1), :]), dma=True)
        transpose_128(ch, n, lambda k: n.xn_f[:, 128 * k:128 * (k + 1)],
                      lambda k: n.xn_t[:, 128 * k:128 * (k + 1)], 8)
        if cfg["topk"]:
            ch.run("sync", lambda e, t=t: e.dma_start(out=n.xnlo_t[:], in_=dr.xglo_d[128 * t:128 * (t + 1), :]), dma=True)
            transpose_128(ch, n, lambda k: n.xnlo_f[:, 128 * k:128 * (k + 1)],
                          lambda k: n.xnlo_t[:, 128 * k:128 * (k + 1)], 8)
        for og in range(3):
            bank = pb[og % 2]
            three = cfg["topk"] and og < 2
            for k in range(8):
                ch.run("t", lambda e, og=og, k=k, bank=bank: e.matmul(
                    out=bank[:], lhsT=n.xn_f[:, 128 * k:128 * (k + 1)],
                    rhs=n.wqkvT[k][:, 512 * og:512 * (og + 1)],
                    start=(k == 0), stop=(k == 7 and not three)))
            if three:
                for k in range(8):
                    ch.run("t", lambda e, og=og, k=k, bank=bank: e.matmul(
                        out=bank[:], lhsT=n.xnlo_f[:, 128 * k:128 * (k + 1)],
                        rhs=n.wqkvT[k][:, 512 * og:512 * (og + 1)],
                        start=False, stop=False))
                for k in range(8):
                    ch.run("t", lambda e, og=og, k=k, bank=bank: e.matmul(
                        out=bank[:], lhsT=n.xn_f[:, 128 * k:128 * (k + 1)],
                        rhs=n.wqkvTlo[k][:, 512 * og:512 * (og + 1)],
                        start=False, stop=(k == 7)))
            sc = qsc if og == 0 else 1.0
            ch.run("v", lambda e, og=og, bank=bank, sc=sc: e.scalar_tensor_tensor(
                out=n.qkv_t[:, 512 * og:512 * (og + 1)], in0=bank[:], scalar=sc,
                in1=n.bqkv_bc[:, 512 * og:512 * (og + 1)], op0=ALU.mult, op1=ALU.add))
        if cfg["rope"]:
            rope_pair(ch, cfg, n, t)
        ch.run("v", lambda e: e.tensor_copy(out=n.qkv_b[:], in_=n.qkv_t[:, 0:2 * MYF]))
        ch.run("v", lambda e: e.tensor_copy(out=n.vc[:], in_=n.qkv_t[:, 2 * MYF:3 * MYF]))
        ch.run("sync", lambda e, t=t: e.dma_start(out=dr.vt_d[128 * t:128 * (t + 1), :], in_=n.vc[:]), dma=True)
        dsts = [(0, n.qkv_b, dr.qf_d), (1, n.qkv_b, dr.kf_d)]
        if cfg["topk"]:
            ch.run("v", lambda e: e.tensor_tensor(out=n.dump[:], in0=n.qkv_t[:, 0:2 * MYF],
                                                  in1=n.qkv_b[:], op=ALU.subtract))
            ch.run("v", lambda e: e.tensor_copy(out=n.qkvlo_b[:], in_=n.dump[:]))
            dsts += [(0, n.qkvlo_b, dr.qflo_d), (1, n.qkvlo_b, dr.kflo_d)]
        for part, srcb, dst in dsts:
            transpose_128(ch, n, lambda k: n.xn_f[:, 128 * k:128 * (k + 1)],
                          lambda k, part=part, srcb=srcb: srcb[:, 512 * part + 128 * k:512 * part + 128 * (k + 1)], 4)
            xf = n.xn_f[:]
            ch.run("sync", lambda e, t=t, dst=dst, xf=xf: e.dma_start(
                out=RawAP(dst.tensor, 128 * t, [[S, 128], [128 * S, 4], [1, 128]]),
                in_=RawAP(xf.tensor, xf.offset, [[xf.ap[0][0], 128], [128, 4], [1, 128]])), dma=True)

    for h in range(MYH):
        ch.run("sync", lambda e, h=h: e.dma_start(out=n.kh[0:64, :], in_=dr.kf_d[64 * h:64 * (h + 1), :]), dma=True)
        if cfg["topk"]:
            ch.run("sync", lambda e, h=h: e.dma_start(out=n.khlo[0:64, :], in_=dr.kflo_d[64 * h:64 * (h + 1), :]), dma=True)
        if cfg["relbias"]:
            ch.run("gp", lambda e, h=h: e.dma_start(
                out=n.skq[:], in_=RawAP(ap["bexr"].tensor, h * LQT, [[1, 128], [1, LQW]])), dma=True)
            ch.run("gp", lambda e, h=h: e.dma_start(
                out=n.skk[:], in_=RawAP(ap["bex"].tensor, h * LQT + 130, [[1, 128], [1, W2K]])), dma=True)
            ch.run("t", lambda e, h=h: e.matmul(out=n.pbc[:, 0:2], lhsT=n.onescol_f[0:1, :],
                                                rhs=n.bfar_r[0:1, 2 * h:2 * h + 2], start=True, stop=True))
            ch.run("v", lambda e: e.tensor_copy(out=n.farv[:], in_=n.pbc[:, 0:2]), small=True)
        for qb in range(8):
            q0 = 512 * qb
            ch.run("sync", lambda e, h=h, q0=q0: e.dma_start(
                out=n.qa[0:64, :], in_=dr.qf_d[64 * h:64 * (h + 1), q0:q0 + 512]), dma=True)
            if cfg["topk"]:
                ch.run("sync", lambda e, h=h, q0=q0: e.dma_start(
                    out=n.qalo[0:64, :], in_=dr.qflo_d[64 * h:64 * (h + 1), q0:q0 + 512]), dma=True)
            emit_head_block(ch, cfg, n, h, qb)

    for qq in range(32):
        transpose_128(ch, n, lambda k: n.off_t[:, 128 * k:128 * (k + 1)],
                      lambda k, qq=qq: n.osb[:, 512 * qq + 128 * k:512 * qq + 128 * (k + 1)], 4)
        for og in range(2):
            bank = pb[og % 2]
            for k in range(4):
                ch.run("t", lambda e, og=og, k=k, bank=bank: e.matmul(
                    out=bank[:], lhsT=n.off_t[:, 128 * k:128 * (k + 1)],
                    rhs=n.woT[k][:, 512 * og:512 * (og + 1)], start=(k == 0), stop=(k == 3)))
            ch.run("v", lambda e, og=og, bank=bank: e.tensor_copy(
                out=n.x_t[:, 512 * og:512 * (og + 1)], in_=bank[:]))
        ch.run("sync", lambda e, qq=qq: e.dma_start(out=dr.y_d[128 * qq:128 * (qq + 1), :], in_=n.x_t[:]), dma=True)

    coll("ReduceScatter", ALU.add, dr.y_d, dr.yr_d)

    for t in range(ROWS // 128):
        ch.run("sync", lambda e, t=t: e.dma_start(out=n.x_t[:], in_=ap["x"][128 * t:128 * (t + 1), :]), dma=True)
        ch.run("sync", lambda e, t=t: e.dma_start(out=n.xb_t[:], in_=dr.yr_d[128 * t:128 * (t + 1), :]), dma=True)
        ch.run("v", lambda e: e.tensor_tensor(out=n.dump[:], in0=n.x_t[:], in1=n.bo_bc[:], op=ALU.add))
        ch.run("v", lambda e: e.tensor_tensor(out=n.x_t[:], in0=n.dump[:], in1=n.xb_t[:], op=ALU.add))
        ch.run("sync", lambda e, t=t: e.dma_start(out=n.xo[128 * t:128 * (t + 1), :], in_=n.x_t[:]), dma=True)
    if cfg.get("dbg"):
        ch.run("sync", lambda e: e.dma_start(out=ap["osb_o"], in_=n.osb[:]), dma=True)
        ch.run("v", lambda e: e.tensor_copy(out=n.dump[:, 0:16], in_=n.cm[:]))
        ch.run("v", lambda e: e.tensor_copy(out=n.dump[:, 16:17], in_=n.Mv[:]))
        ch.run("v", lambda e: e.tensor_copy(out=n.dump[:, 17:21], in_=n.rz[:]))
        if cfg["topk"]:
            ch.run("v", lambda e: e.tensor_copy(out=n.dump[:, 21:22], in_=n.lov[:]))
            ch.run("v", lambda e: e.tensor_copy(out=n.dump[:, 22:23], in_=n.cntv[:]))
            ch.run("v", lambda e: e.tensor_copy(out=n.dump[:, 23:24], in_=n.Mv[:]))
            ch.run("v", lambda e: e.tensor_copy(out=n.dump[:, 24:25], in_=n.hiv[:]))
        ch.run("sync", lambda e: e.dma_start(out=ap["dbg_o"], in_=n.dump[:, 0:64]), dma=True)
    ch.run("gp", lambda e: e.memset(n.sm[:, 8:9], 0.0))


def rope_pair(ch, cfg, n, t):
    ap = n.ap
    ch.run("sync", lambda e, t=t: e.dma_start(out=n.rc_t[:], in_=ap["ropec"][128 * t:128 * (t + 1), :]), dma=True)
    ch.run("sync", lambda e, t=t: e.dma_start(out=n.rs_t[:], in_=ap["ropes"][128 * t:128 * (t + 1), :]), dma=True)
    cosb = n.rc_t[:].unsqueeze(1).broadcast_to([128, MYH, 32])
    sinb = n.rs_t[:].unsqueeze(1).broadcast_to([128, MYH, 32])
    for part in range(2):
        base = 512 * part
        xe = n.qkv_t[:, base:base + 512:2].rearrange("p (a b) -> p a b", a=MYH)
        xo_ = n.qkv_t[:, base + 1:base + 512:2].rearrange("p (a b) -> p a b", a=MYH)
        te = n.rtmp[:, 0:256].rearrange("p (a b) -> p a b", a=MYH)
        to = n.rtmp[:, 256:512].rearrange("p (a b) -> p a b", a=MYH)
        ch.run("v", lambda e, xe=xe, cosb=cosb, te=te: e.tensor_tensor(out=te, in0=xe, in1=cosb, op=ALU.mult))
        ch.run("v", lambda e, xo_=xo_, sinb=sinb, to=to: e.tensor_tensor(out=to, in0=xo_, in1=sinb, op=ALU.mult))
        ch.run("v", lambda e, te=te, to=to: e.tensor_tensor(out=te, in0=te, in1=to, op=ALU.subtract), fence=True)
        ch.run("v", lambda e, xe=xe, sinb=sinb, to=to: e.tensor_tensor(out=to, in0=xe, in1=sinb, op=ALU.mult))
        ch.run("v", lambda e, xo_=xo_, cosb=cosb: e.tensor_tensor(out=xo_, in0=xo_, in1=cosb, op=ALU.mult), fence=True)
        ch.run("v", lambda e, xo_=xo_, to=to: e.tensor_tensor(out=xo_, in0=to, in1=xo_, op=ALU.add), fence=True)
        ch.run("v", lambda e, xe=xe, te=te: e.tensor_copy(out=xe, in_=te))


def put_neg_m_row(ch, n, s_):
    ch.run("v", lambda e: e.tensor_scalar(out=n.mrow_b[:], in0=n.Mv[:], scalar1=-1.0,
                                          scalar2=None, op0=ALU.mult), small=True)
    ch.run("t", lambda e: e.transpose(out=n.pt_bf[0:1, 0:128], in_=n.mrow_b[:], identity=n.ident[:]))
    ch.run("v", lambda e, s_=s_: e.tensor_copy(out=n.qa[64:65, 128 * s_:128 * (s_ + 1)],
                                               in_=n.pt_bf[0:1, 0:128]), small=True)


def emit_head_block(ch, cfg, n, h, qb):
    pb = n.pb
    causal = cfg["causal"]
    q0 = 512 * qb
    for s_ in range(4):
        qa0 = q0 + 128 * s_
        if cfg["topk"]:
            emit_l1a_stats(ch, cfg, n, h, qb, s_)
            continue
        nk = (qa0 + 128 + 511) // 512 if causal else 8
        for kc in range(nk):
            bank = pb[kc % 2]
            ch.run("t", lambda e, kc=kc, s_=s_, bank=bank: e.matmul(
                out=bank[:], lhsT=n.qa[0:64, 128 * s_:128 * (s_ + 1)],
                rhs=n.kh[0:64, 512 * kc:512 * kc + 512], start=True, stop=True))
            isdiag = causal and (512 * kc <= qa0 < 512 * (kc + 1))
            tri = n.triq[(qa0 - 512 * kc) // 128] if isdiag else None
            if cfg["decay"]:
                if isdiag:
                    ch.run("v", lambda e, bank=bank, tri=tri: e.scalar_tensor_tensor(
                        out=n.stmp[:], in0=bank[:], scalar=1.0, in1=tri[:], op0=ALU.mult, op1=ALU.add))
                    ch.run("v", lambda e: e.tensor_tensor(out=n.stmp[:], in0=n.stmp[:],
                                                          in1=n.poskq[:], op=ALU.add), fence=True)
                else:
                    ch.run("v", lambda e, bank=bank: e.scalar_tensor_tensor(
                        out=n.stmp[:], in0=bank[:], scalar=1.0, in1=n.poskq[:], op0=ALU.mult, op1=ALU.add))
                ch.run("v", lambda e, kc=kc: e.reduce_max(out=n.cm[:, kc:kc + 1], in_=n.stmp[:], axis=AX.X), small=True)
            elif isdiag:
                ch.run("v", lambda e, bank=bank, tri=tri: e.scalar_tensor_tensor(
                    out=n.stmp[:], in0=bank[:], scalar=1.0, in1=tri[:], op0=ALU.mult, op1=ALU.add))
                ch.run("v", lambda e, kc=kc: e.reduce_max(out=n.cm[:, kc:kc + 1], in_=n.stmp[:], axis=AX.X), small=True)
            else:
                ch.run("v", lambda e, kc=kc, bank=bank: e.reduce_max(out=n.cm[:, kc:kc + 1], in_=bank[:], axis=AX.X), small=True)
        if cfg["decay"]:
            ch.run("v", lambda e, nk=nk: e.tensor_tensor(out=n.cm[:, 0:nk], in0=n.cm[:, 0:nk],
                                                         in1=n.kcoff[:, 0:nk], op=ALU.add), small=True, fence=True)
        ch.run("v", lambda e, nk=nk: e.reduce_max(out=n.Mv[:], in_=n.cm[:, 0:nk], axis=AX.X), small=True)
        put_neg_m_row(ch, n, s_)

    nkc = (q0 + 512) // 128 if causal else 32
    for kc in range(nkc):
        bank = pb[kc % 2]
        if cfg["topk"]:
            ch.run("t", lambda e, kc=kc, bank=bank: e.matmul(
                out=bank[:], lhsT=n.kh[0:65, 128 * kc:128 * (kc + 1)], rhs=n.qa[0:65, :],
                start=True, stop=False))
            ch.run("t", lambda e, kc=kc, bank=bank: e.matmul(
                out=bank[:], lhsT=n.khlo[0:65, 128 * kc:128 * (kc + 1)], rhs=n.qa[0:65, :],
                start=False, stop=False))
            ch.run("t", lambda e, kc=kc, bank=bank: e.matmul(
                out=bank[:], lhsT=n.kh[0:65, 128 * kc:128 * (kc + 1)], rhs=n.qalo[0:65, :],
                start=False, stop=True))
        else:
            ch.run("t", lambda e, kc=kc, bank=bank: e.matmul(
                out=bank[:], lhsT=n.kh[0:65, 128 * kc:128 * (kc + 1)], rhs=n.qa[0:65, :],
                start=True, stop=True))
        bias = n.poskk[:, kc:kc + 1] if cfg["decay"] else 0.0
        dec_diag = cfg["decay"] and causal and 128 * kc >= q0
        if cfg["topk"]:
            dkq = 128 * kc - q0
            if -638 <= dkq <= 1022:
                v0k = 1149 + dkq
                skk = n.skk[:]
                srck = RawAP(skk.tensor, skk.offset + v0k, [[skk.ap[0][0], 128], [-1, 512]])
                ch.run("v", lambda e, bank=bank, srck=srck: e.scalar_tensor_tensor(
                    out=n.stmp[:], in0=bank[:], scalar=1.0, in1=srck, op0=ALU.mult, op1=ALU.add))
            else:
                fcol = 1 if dkq > 1022 else 0
                ch.run("v", lambda e, bank=bank, fcol=fcol: e.tensor_scalar(
                    out=n.stmp[:], in0=bank[:], scalar1=n.farv[:, fcol:fcol + 1],
                    scalar2=None, op0=ALU.add))
            ch.run("s", lambda e: e.activation(out=n.PT[:], in_=n.stmp[:], func=AF.Exp,
                                               bias=0.0, scale=1.0))
            ch.run("v", lambda e: e.tensor_tensor(out=n.tjunk[:, 0:512], in0=n.stmp[:],
                                                  in1=n.taub[:], op=ALU.is_ge))
            if cfg.get("mask_off"):
                cur = n.PT
            else:
                ch.run("v", lambda e: e.tensor_tensor(out=n.PT2[:], in0=n.PT[:],
                                                      in1=n.tjunk[:, 0:512], op=ALU.mult))
                cur = n.PT2
        elif dec_diag:
            d_ = kc - q0 // 128
            ch.run("v", lambda e, bank=bank, d_=d_: e.scalar_tensor_tensor(
                out=n.stmp[:], in0=bank[:], scalar=1.0, in1=n.trikq[d_][:],
                op0=ALU.mult, op1=ALU.add))
            ch.run("s", lambda e, bias=bias: e.activation(
                out=n.PT[:], in_=n.stmp[:], func=AF.Exp, bias=bias, scale=1.0))
            cur = n.PT
        else:
            ch.run("s", lambda e, bank=bank, bias=bias: e.activation(
                out=n.PT[:], in_=bank[:], func=AF.Exp, bias=bias, scale=1.0))
            cur = n.PT
        if causal and 128 * kc >= q0 and not dec_diag:
            d_ = kc - q0 // 128
            dst = n.PT2 if cur is n.PT else n.PT
            ch.run("v", lambda e, cur=cur, dst=dst, d_=d_: e.tensor_tensor(
                out=dst[:], in0=cur[:], in1=n.trik[d_][:], op=ALU.mult))
            cur = dst
        ch.run("sync", lambda e, kc=kc: e.dma_start(out=n.vc[:], in_=n.dr.vt_d[128 * kc:128 * (kc + 1), :]), dma=True)
        for s_ in range(4):
            ob = pb[2 + s_]
            ch.run("t", lambda e, s_=s_, kc=kc, cur=cur, ob=ob, h=h: e.matmul(
                out=ob[:, 0:64], lhsT=cur[:, 128 * s_:128 * (s_ + 1)],
                rhs=n.vc[:, 64 * h:64 * (h + 1)], start=(kc == 0), stop=False))
            ch.run("t", lambda e, s_=s_, kc=kc, cur=cur, ob=ob, nkc=nkc: e.matmul(
                out=ob[:, 64:65], lhsT=cur[:, 128 * s_:128 * (s_ + 1)],
                rhs=n.onescol_b[:], start=False, stop=(kc == nkc - 1)))
    for s_ in range(4):
        ob = pb[2 + s_]
        qq = 4 * qb + s_
        ch.run("v", lambda e, ob=ob, s_=s_: e.reciprocal(out=n.rz[:, s_:s_ + 1], in_=ob[:, 64:65]), small=True)
        if cfg["gates"]:
            ch.run("v", lambda e, s_=s_, h=h: e.tensor_tensor(
                out=n.rz[:, s_:s_ + 1], in0=n.rz[:, s_:s_ + 1], in1=n.gate_bc[:, h:h + 1],
                op=ALU.mult), small=True, fence=True)
        ch.run("v", lambda e, ob=ob, s_=s_, qq=qq, h=h: e.tensor_scalar(
            out=n.osb[:, 512 * qq + 64 * h:512 * qq + 64 * (h + 1)], in0=ob[:, 0:64],
            scalar1=n.rz[:, s_:s_ + 1], scalar2=None, op0=ALU.mult), small=True)


def emit_l1a_stats(ch, cfg, n, h, qb, s_):
    pb = n.pb
    q0 = 512 * qb
    qa0 = q0 + 128 * s_
    for kc in range(8):
        bank = pb[kc % 2]
        ch.run("t", lambda e, kc=kc, s_=s_, bank=bank: e.matmul(
            out=bank[:], lhsT=n.qa[0:64, 128 * s_:128 * (s_ + 1)],
            rhs=n.kh[0:64, 512 * kc:512 * kc + 512], start=True, stop=False))
        ch.run("t", lambda e, kc=kc, s_=s_, bank=bank: e.matmul(
            out=bank[:], lhsT=n.qalo[0:64, 128 * s_:128 * (s_ + 1)],
            rhs=n.kh[0:64, 512 * kc:512 * kc + 512], start=False, stop=False))
        ch.run("t", lambda e, kc=kc, s_=s_, bank=bank: e.matmul(
            out=bank[:], lhsT=n.qa[0:64, 128 * s_:128 * (s_ + 1)],
            rhs=n.khlo[0:64, 512 * kc:512 * kc + 512], start=False, stop=True))
        dk = 512 * kc - qa0
        if -1152 <= dk <= 640:
            v0 = TMAX - dk
            skq = n.skq[:]
            src = RawAP(skq.tensor, skq.offset + v0, [[skq.ap[0][0], 128], [-1, 512]])
            ch.run("v", lambda e, bank=bank, src=src, kc=kc: e.scalar_tensor_tensor(
                out=n.ssb[:, 512 * kc:512 * (kc + 1)], in0=bank[:], scalar=1.0,
                in1=src, op0=ALU.mult, op1=ALU.add))
        else:
            fcol = 1 if dk > 640 else 0
            ch.run("v", lambda e, bank=bank, kc=kc, fcol=fcol: e.tensor_scalar(
                out=n.ssb[:, 512 * kc:512 * (kc + 1)], in0=bank[:],
                scalar1=n.farv[:, fcol:fcol + 1], scalar2=None, op0=ALU.add))
    ch.run("v", lambda e: e.tensor_reduce(out=n.segm[:], in_=n.ssb[:].rearrange("p (a b) -> p a b", a=64),
                                          axis=AX.X, op=ALU.max), small=True)
    ch.run("v", lambda e: e.reduce_max(out=n.Mv[:], in_=n.segm[:], axis=AX.X), small=True)
    ch.run("v", lambda e: e.tensor_reduce(out=n.lov[:], in_=n.segm[:], axis=AX.X, op=ALU.min), small=True)
    ch.run("v", lambda e: e.tensor_copy(out=n.hiv[:], in_=n.Mv[:]), small=True)
    for it in range(NITER_TOPK):
        ch.run("v", lambda e: e.tensor_tensor(out=n.midv[:], in0=n.lov[:], in1=n.hiv[:], op=ALU.add), small=True)
        ch.run("v", lambda e: e.tensor_scalar(out=n.sm[:, 9:10], in0=n.midv[:], scalar1=0.5,
                                              scalar2=None, op0=ALU.mult), small=True)
        ch.run("v", lambda e: e.tensor_copy(out=n.midv[:], in_=n.sm[:, 9:10]), small=True)
        ch.run("v", lambda e: e.tensor_scalar(out=n.tjunk[:], in0=n.ssb[:], scalar1=n.midv[:],
                                              scalar2=None, op0=ALU.is_ge, op1=ALU.add,
                                              accum_out=n.cntv[:]), small=True)
        ch.run("v", lambda e: e.tensor_scalar(out=n.mskv[:], in0=n.cntv[:], scalar1=64.0,
                                              scalar2=None, op0=ALU.is_ge), small=True)
        ch.run("v", lambda e: e.copy_predicated(out=n.lov[:], mask=n.mskv[:], data=n.midv[:]), small=True, fence=True)
        ch.run("v", lambda e: e.tensor_scalar(out=n.mskv[:], in0=n.cntv[:], scalar1=64.0,
                                              scalar2=None, op0=ALU.is_lt), small=True)
        ch.run("v", lambda e: e.copy_predicated(out=n.hiv[:], mask=n.mskv[:], data=n.midv[:]), small=True, fence=True)
    ch.run("v", lambda e: e.tensor_scalar(out=n.tjunk[:], in0=n.ssb[:], scalar1=n.lov[:],
                                          scalar2=None, op0=ALU.is_ge, op1=ALU.add,
                                          accum_out=n.cntv[:]), small=True)
    ch.run("v", lambda e: e.tensor_scalar(out=n.mrow_b[:], in0=n.Mv[:], scalar1=-1.0,
                                          scalar2=None, op0=ALU.mult), small=True)
    ch.run("v", lambda e: e.tensor_copy(out=n.Mb32[:], in_=n.mrow_b[:]), small=True)
    ch.run("v", lambda e: e.tensor_tensor(out=n.sm[:, 10:11], in0=n.lov[:], in1=n.Mb32[:], op=ALU.add), small=True)
    ch.run("t", lambda e: e.transpose(out=n.pbc[0:1, 0:128], in_=n.sm[:, 10:11], identity=n.identf[:]))
    ch.run("v", lambda e, s_=s_: e.tensor_copy(out=n.taur_f[0:1, 128 * s_:128 * (s_ + 1)],
                                               in_=n.pbc[0:1, 0:128]), small=True)
    ch.run("t", lambda e: e.transpose(out=n.pt_bf[0:1, 0:128], in_=n.mrow_b[:], identity=n.ident[:]))
    ch.run("v", lambda e, s_=s_: e.tensor_copy(out=n.qa[64:65, 128 * s_:128 * (s_ + 1)],
                                               in_=n.pt_bf[0:1, 0:128]), small=True)
    if s_ == 3:
        ch.run("t", lambda e: e.matmul(out=n.pbc[:], lhsT=n.onescol_f[0:1, :],
                                       rhs=n.taur_f[0:1, :], start=True, stop=True))
        ch.run("v", lambda e: e.tensor_copy(out=n.taub[:], in_=n.pbc[:]))



def build_ffn():
    nc = bass.Bass("TRN2", target_bir_lowering=False)
    ap = {}
    ap["x"] = nc.dram_tensor("x", [ROWS, D], F32, kind="ExternalInput").ap()
    ap["w"] = nc.dram_tensor("w", [D, D], F32, kind="ExternalInput").ap()
    ap["ident"] = nc.dram_tensor("ident", [128, 128], BF16, kind="ExternalInput").ap()
    xo = nc.dram_tensor("xo", [ROWS, D], F32, kind="ExternalOutput").ap()
    st = ExitStack()
    with st:
        ch = Ch(nc)
        ch.open_sems(st)
        ffn_stage(nc, ch, st, ap, xo, "")
        with nc.Block() as block:
            ch.flush(block)
    return nc


def ffn_stage(nc, ch, stage_st, ap, xo, pfx):
    st = stage_st
    if True:
        n = NS()
        def sb(name, shape, dt):
            t = st.enter_context(nc.sbuf_tensor(pfx + "sb_" + name, shape, dt))
            n[name] = t
            return t
        sb("ident", [128, 128], BF16)
        n.wsT = [st.enter_context(nc.sbuf_tensor(pfx + f"wsT{k}", [128, D], BF16)) for k in range(8)]
        sb("wtmp_f", [128, D], F32); sb("wtmp_b", [128, D], BF16)
        sb("x_t", [128, D], F32); sb("dump", [128, D], F32)
        sb("sm", [128, 16], F32); sb("epsv", [128, 1], F32)
        sb("xn_t", [128, D], BF16); sb("xn_f", [128, D], BF16)
        n.pb = [st.enter_context(nc.psum_tensor(pfx + f"pb{i}", [128, 512], F32)) for i in range(2)]
        n.pt_bf = st.enter_context(nc.psum_tensor(pfx + "pt_bf", [128, 1024], BF16))
        n.ap = ap
        ch.run("v", lambda e: e.memset(n.epsv[:], float(EPS)), small=True)
        ch.run("sync", lambda e: e.dma_start(out=n.ident[:], in_=ap["ident"]), dma=True)
        load_wT(ch, n, ap["w"], n.wsT, D, sign=True)
        for t in range(ROWS // 128):
            ch.run("sync", lambda e, t=t: e.dma_start(out=n.x_t[:], in_=ap["x"][128 * t:128 * (t + 1), :]), dma=True)
            ln_tile(ch, n, n.x_t[:], n.xn_t[:])
            transpose_128(ch, n, lambda k: n.xn_f[:, 128 * k:128 * (k + 1)],
                          lambda k: n.xn_t[:, 128 * k:128 * (k + 1)], 8)
            for og in range(2):
                bank = n.pb[og]
                for k in range(8):
                    ch.run("t", lambda e, og=og, k=k, bank=bank: e.matmul(
                        out=bank[:], lhsT=n.xn_f[:, 128 * k:128 * (k + 1)],
                        rhs=n.wsT[k][:, 512 * og:512 * (og + 1)], start=(k == 0), stop=(k == 7)))
                ch.run("v", lambda e, og=og, bank=bank: e.scalar_tensor_tensor(
                    out=n.x_t[:, 512 * og:512 * (og + 1)], in0=bank[:], scalar=1.0,
                    in1=n.x_t[:, 512 * og:512 * (og + 1)], op0=ALU.mult, op1=ALU.add), fence=True)
            ch.run("sync", lambda e, t=t: e.dma_start(out=xo[128 * t:128 * (t + 1), :], in_=n.x_t[:]), dma=True)
        ch.run("gp", lambda e: e.memset(n.sm[:, 8:9], 0.0))



def build_moe():
    nc = bass.Bass("TRN2", target_bir_lowering=False)
    ap = {}
    def din(name, shape, dt=F32):
        ap[name] = nc.dram_tensor(name, shape, dt, kind="ExternalInput").ap()
    din("x", [ROWS, D])
    din("wgs", [9, 2048, D])
    din("wus", [9, 2048, D])
    din("wds", [9, D, 2048])
    din("rw", [8, D])
    din("rb", [1, 8])
    din("ident", [128, 128], BF16)
    xo = nc.dram_tensor("xo", [ROWS, D], F32, kind="ExternalOutput").ap()
    st = ExitStack()
    with st:
        ch = Ch(nc)
        ch.open_sems(st)
        moe_stage(nc, ch, st, ap, xo, "")
        with nc.Block() as block:
            ch.flush(block)
    return nc


def moe_stage(nc, ch, st, ap, xo, pfx):
    xn_d = nc.dram_tensor(pfx + "xn_d", [ROWS, D], BF16).ap()
    if True:
        n = NS()
        def sb(name, shape, dt):
            t = st.enter_context(nc.sbuf_tensor(pfx + "sb_" + name, shape, dt))
            n[name] = t
            return t
        n.xo_ap = xo
        n.xn_d_ap = xn_d
        sb("ident", [128, 128], BF16)
        sb("wtmp_f", [128, D], F32); sb("wtmp_b", [128, 2048], BF16)
        sb("x_t", [128, D], F32); sb("dump", [128, D], F32)
        sb("sm", [128, 16], F32); sb("epsv", [128, 1], F32)
        sb("xn_t", [128, D], BF16)
        n.xnf = [st.enter_context(nc.sbuf_tensor(pfx + f"xnf{k}", [128, 1024], BF16)) for k in range(8)]
        n.A1 = [st.enter_context(nc.sbuf_tensor(pfx + f"A1_{k}", [128, 2048], BF16)) for k in range(8)]
        n.A2 = [st.enter_context(nc.sbuf_tensor(pfx + f"A2_{k}", [128, D], BF16)) for k in range(16)]
        n.gs = [st.enter_context(nc.sbuf_tensor(pfx + f"gs_{k}", [128, 1024], BF16)) for k in range(16)]
        n.ya = [st.enter_context(nc.sbuf_tensor(pfx + f"ya_{k}", [128, D], F32)) for k in range(8)]
        n.rwT = [st.enter_context(nc.sbuf_tensor(pfx + f"rwT{k}", [128, 8], BF16)) for k in range(8)]
        n.rwTlo = [st.enter_context(nc.sbuf_tensor(pfx + f"rwTlo{k}", [128, 8], BF16)) for k in range(8)]
        sb("xn32", [128, D], F32); sb("xnlo_t", [128, D], BF16)
        sb("rb_bc", [128, 8], F32); sb("rb_r", [1, 8], F32)
        sb("onescol_f", [1, 128], F32)
        sb("probs", [128, 8], F32); sb("p2", [128, 8], F32)
        sb("w8", [128, 8], F32)
        sb("mx", [128, 4], F32)
        sb("wc", [128, 9 * 16], F32)
        n.pb = [st.enter_context(nc.psum_tensor(pfx + f"pb{i}", [128, 512], F32)) for i in range(4)]
        n.pbc = st.enter_context(nc.psum_tensor(pfx + "pbc", [128, 512], F32))
        n.pt_bf = st.enter_context(nc.psum_tensor(pfx + "pt_bf", [128, 2048], BF16))
        n.ap = ap
        emit_moe(ch, n)


def emit_moe(ch, n):
    ap = n.ap
    xo = None
    ch.run("v", lambda e: e.memset(n.epsv[:], float(EPS)), small=True)
    ch.run("v", lambda e: e.memset(n.onescol_f[:], 1.0), small=True)
    ch.run("sync", lambda e: e.dma_start(out=n.ident[:], in_=ap["ident"]), dma=True)
    ch.run("sync", lambda e: e.dma_start(out=n.rb_r[:], in_=ap["rb"]), dma=True)
    bcast_row(ch, n, n.rb_bc, n.rb_r[:], 8)
    ch.run("sync", lambda e: e.dma_start(out=n.wtmp_f[0:8, 0:D], in_=ap["rw"]), dma=True)
    ch.run("v", lambda e: e.tensor_copy(out=n.wtmp_b[0:8, 0:D], in_=n.wtmp_f[0:8, 0:D]))
    ch.run("v", lambda e: e.tensor_tensor(out=n.dump[0:8, 0:D], in0=n.wtmp_f[0:8, 0:D],
                                          in1=n.wtmp_b[0:8, 0:D], op=ALU.subtract))
    ch.run("v", lambda e: e.tensor_copy(out=n.wtmp_b[0:8, D:2 * D], in_=n.dump[0:8, 0:D]), fence=True)
    for k in range(8):
        ch.run("t", lambda e, k=k: e.transpose(out=n.pt_bf[:, 0:8], in_=n.wtmp_b[0:8, 128 * k:128 * (k + 1)],
                                               identity=n.ident[0:8, 0:8]))
        ch.run("v", lambda e, k=k: e.tensor_copy(out=n.rwT[k][:], in_=n.pt_bf[:, 0:8]), small=True)
        ch.run("t", lambda e, k=k: e.transpose(out=n.pt_bf[:, 0:8], in_=n.wtmp_b[0:8, D + 128 * k:D + 128 * (k + 1)],
                                               identity=n.ident[0:8, 0:8]))
        ch.run("v", lambda e, k=k: e.tensor_copy(out=n.rwTlo[k][:], in_=n.pt_bf[:, 0:8]), small=True)
    for t in range(ROWS // 128):
        ch.run("sync", lambda e, t=t: e.dma_start(out=n.x_t[:], in_=ap["x"][128 * t:128 * (t + 1), :]), dma=True)
        ln_tile(ch, n, n.x_t[:], n.xn32[:])
        ch.run("v", lambda e: e.tensor_copy(out=n.xn_t[:], in_=n.xn32[:]))
        ch.run("v", lambda e: e.tensor_tensor(out=n.dump[:], in0=n.xn32[:], in1=n.xn_t[:], op=ALU.subtract))
        ch.run("v", lambda e: e.tensor_copy(out=n.xnlo_t[:], in_=n.dump[:]), fence=True)
        ch.run("sync", lambda e, t=t: e.dma_start(out=n.xn_d_ap[128 * t:128 * (t + 1), :], in_=n.xn_t[:]), dma=True)
        transpose_128(ch, n, lambda k: n.xnf[k][:, 0:128], lambda k: n.xn_t[:, 128 * k:128 * (k + 1)], 8)
        transpose_128(ch, n, lambda k: n.wtmp_b[:, 128 * k:128 * (k + 1)],
                      lambda k: n.xnlo_t[:, 128 * k:128 * (k + 1)], 8)
        bank = n.pbc
        for k in range(8):
            ch.run("t", lambda e, k=k, bank=bank: e.matmul(
                out=bank[:, 0:8], lhsT=n.xnf[k][:, 0:128], rhs=n.rwT[k][:], start=(k == 0), stop=False))
        for k in range(8):
            ch.run("t", lambda e, k=k, bank=bank: e.matmul(
                out=bank[:, 0:8], lhsT=n.wtmp_b[:, 128 * k:128 * (k + 1)], rhs=n.rwT[k][:], start=False, stop=False))
        for k in range(8):
            ch.run("t", lambda e, k=k, bank=bank: e.matmul(
                out=bank[:, 0:8], lhsT=n.xnf[k][:, 0:128], rhs=n.rwTlo[k][:], start=False, stop=(k == 7)))
        ch.run("v", lambda e, bank=bank: e.scalar_tensor_tensor(
            out=n.probs[:], in0=bank[:, 0:8], scalar=1.0, in1=n.rb_bc[:], op0=ALU.mult, op1=ALU.add), small=True)
        ch.run("s", lambda e: e.activation(out=n.probs[:], in_=n.probs[:], func=AF.Sigmoid,
                                           bias=0.0, scale=1.0), small=True, fence=True)
        ch.run("v", lambda e: e.reduce_max(out=n.mx[:, 0:1], in_=n.probs[:], axis=AX.X), small=True)
        ch.run("v", lambda e: e.tensor_scalar(out=n.p2[:], in0=n.probs[:], scalar1=n.mx[:, 0:1],
                                              scalar2=None, op0=ALU.is_ge), small=True)
        ch.run("v", lambda e: e.scalar_tensor_tensor(out=n.p2[:], in0=n.p2[:], scalar=-1e30,
                                                     in1=n.probs[:], op0=ALU.mult, op1=ALU.add), small=True, fence=True)
        ch.run("v", lambda e: e.reduce_max(out=n.mx[:, 1:2], in_=n.p2[:], axis=AX.X), small=True)
        ch.run("v", lambda e: e.tensor_tensor(out=n.mx[:, 2:3], in0=n.mx[:, 0:1], in1=n.mx[:, 1:2], op=ALU.add), small=True)
        ch.run("v", lambda e: e.reciprocal(out=n.mx[:, 3:4], in_=n.mx[:, 2:3]), small=True)
        ch.run("v", lambda e: e.tensor_scalar(out=n.w8[:], in0=n.probs[:], scalar1=n.mx[:, 1:2],
                                              scalar2=None, op0=ALU.is_ge), small=True)
        ch.run("v", lambda e: e.tensor_tensor(out=n.w8[:], in0=n.w8[:], in1=n.probs[:], op=ALU.mult), small=True, fence=True)
        ch.run("v", lambda e: e.tensor_scalar(out=n.w8[:], in0=n.w8[:], scalar1=n.mx[:, 3:4],
                                              scalar2=None, op0=ALU.mult), small=True, fence=True)
        ch.run("v", lambda e, t=t: e.tensor_copy(out=n.wc[:, 9 * t:9 * t + 8], in_=n.w8[:]), small=True)
        ch.run("v", lambda e, t=t: e.memset(n.wc[:, 9 * t + 8:9 * t + 9], 1.0), small=True)
    for half in range(2):
        tb = 8 * half
        for t in range(8):
            ch.run("sync", lambda e, t=t, tb=tb: e.dma_start(out=n.xn_t[:], in_=n.xn_d_ap[128 * (tb + t):128 * (tb + t + 1), :]), dma=True)
            transpose_128(ch, n, lambda k, t=t: n.xnf[k][:, 128 * t:128 * (t + 1)],
                          lambda k: n.xn_t[:, 128 * k:128 * (k + 1)], 8)
            ch.run("sync", lambda e, t=t, tb=tb: e.dma_start(out=n.ya[t][:], in_=ap["x"][128 * (tb + t):128 * (tb + t + 1), :]), dma=True)
        for ex in range(9):
            for j in range(16):
                ch.run("sync", lambda e, ex=ex, j=j: e.dma_start(out=n.wtmp_f[:], in_=ap["wgs"][ex, 128 * j:128 * (j + 1), :]), dma=True)
                ch.run("v", lambda e: e.tensor_copy(out=n.wtmp_b[:, 0:D], in_=n.wtmp_f[:]))
                transpose_128(ch, n, lambda k, j=j: n.A1[k][:, 128 * j:128 * (j + 1)],
                              lambda k: n.wtmp_b[:, 128 * k:128 * (k + 1)], 8)
            for ic in range(16):
                for tb2 in range(2):
                    bank = n.pb[tb2]
                    for k in range(8):
                        ch.run("t", lambda e, ic=ic, tb2=tb2, k=k, bank=bank: e.matmul(
                            out=bank[:], lhsT=n.A1[k][:, 128 * ic:128 * (ic + 1)],
                            rhs=n.xnf[k][:, 512 * tb2:512 * (tb2 + 1)], start=(k == 0), stop=(k == 7)))
                    ch.run("s", lambda e, ic=ic, tb2=tb2, bank=bank: e.activation(
                        out=n.gs[ic][:, 512 * tb2:512 * (tb2 + 1)], in_=bank[:], func=AF.Silu,
                        bias=0.0, scale=1.0))
            for j in range(16):
                ch.run("sync", lambda e, ex=ex, j=j: e.dma_start(out=n.wtmp_f[:], in_=ap["wus"][ex, 128 * j:128 * (j + 1), :]), dma=True)
                ch.run("v", lambda e: e.tensor_copy(out=n.wtmp_b[:, 0:D], in_=n.wtmp_f[:]))
                transpose_128(ch, n, lambda k, j=j: n.A1[k][:, 128 * j:128 * (j + 1)],
                              lambda k: n.wtmp_b[:, 128 * k:128 * (k + 1)], 8)
            for ic in range(16):
                for tb2 in range(2):
                    bank = n.pb[tb2]
                    for k in range(8):
                        ch.run("t", lambda e, ic=ic, tb2=tb2, k=k, bank=bank: e.matmul(
                            out=bank[:], lhsT=n.A1[k][:, 128 * ic:128 * (ic + 1)],
                            rhs=n.xnf[k][:, 512 * tb2:512 * (tb2 + 1)], start=(k == 0), stop=(k == 7)))
                    ch.run("v", lambda e, ic=ic, tb2=tb2, bank=bank: e.tensor_tensor(
                        out=n.gs[ic][:, 512 * tb2:512 * (tb2 + 1)], in0=bank[:],
                        in1=n.gs[ic][:, 512 * tb2:512 * (tb2 + 1)], op=ALU.mult), fence=True)
            for j in range(8):
                ch.run("sync", lambda e, ex=ex, j=j: e.dma_start(out=n.wtmp_f[:], in_=ap["wds"][ex, 128 * j:128 * (j + 1), 0:1024]), dma=True)
                ch.run("sync", lambda e, ex=ex, j=j: e.dma_start(out=n.dump[:], in_=ap["wds"][ex, 128 * j:128 * (j + 1), 1024:2048]), dma=True)
                ch.run("v", lambda e: e.tensor_copy(out=n.wtmp_b[:, 0:D], in_=n.wtmp_f[:]))
                ch.run("v", lambda e: e.tensor_copy(out=n.wtmp_b[:, D:2048], in_=n.dump[:]))
                transpose_128(ch, n, lambda k, j=j: n.A2[k][:, 128 * j:128 * (j + 1)],
                              lambda k: n.wtmp_b[:, 128 * k:128 * (k + 1)], 16)
            for t in range(8):
                for og in range(2):
                    bank = n.pb[og]
                    for ic in range(16):
                        ch.run("t", lambda e, t=t, og=og, ic=ic, bank=bank: e.matmul(
                            out=bank[:], lhsT=n.gs[ic][:, 128 * t:128 * (t + 1)],
                            rhs=n.A2[ic][:, 512 * og:512 * (og + 1)], start=(ic == 0), stop=(ic == 15)))
                    ch.run("v", lambda e, t=t, og=og, ex=ex, tb=tb, bank=bank: e.scalar_tensor_tensor(
                        out=n.ya[t][:, 512 * og:512 * (og + 1)], in0=bank[:],
                        scalar=n.wc[:, 9 * (tb + t) + ex:9 * (tb + t) + ex + 1],
                        in1=n.ya[t][:, 512 * og:512 * (og + 1)], op0=ALU.mult, op1=ALU.add), fence=True)
        for t in range(8):
            ch.run("sync", lambda e, t=t, tb=tb: e.dma_start(out=n.xo_ap[128 * (tb + t):128 * (tb + t + 1), :], in_=n.ya[t][:]), dma=True)
    ch.run("gp", lambda e: e.memset(n.sm[:, 8:9], 0.0))



def build_fused(cfgs, dbg_mid=False):
    nc = bass.Bass("TRN2", target_bir_lowering=False)
    g = {}
    def din(name, shape, dt=F32):
        if name not in g:
            g[name] = nc.dram_tensor(name, shape, dt, kind="ExternalInput").ap()
        return g[name]
    x_in = nc.dram_tensor("x", [ROWS, D], F32, kind="ExternalInput").ap()
    xo = nc.dram_tensor("xo", [ROWS, D], F32, kind="ExternalOutput").ap()
    xd = [nc.dram_tensor(f"xmid{i}", [ROWS, D], F32, kind="ExternalOutput" if dbg_mid else "Internal").ap() for i in range(7)]
    din("ident", [128, 128], BF16)
    din("identf", [128, 128])
    for s_ in range(4):
        din(f"triq_{s_}", [128, 512])
        din(f"trik_{s_}", [128, 512], BF16)
        din(f"trikq_{s_}", [128, 512])
    din("poskq", [128, 512]); din("poskk", [128, 32])
    din("ropec", [S, DH // 2]); din("ropes", [S, DH // 2])

    stage_list = [("a0_", "attn", "l0a"), ("f0_", "ffn", None), ("b0_", "attn", "l0b"),
                  ("f1_", "ffn", None), ("c0_", "attn", "l1a"), ("m0_", "moe", None),
                  ("d0_", "attn", "l1b"), ("m1_", "moe", None)]
    with ExitStack() as top:
        ch = Ch(nc)
        ch.open_sems(top)
        for i, (pfx, kind, ckey) in enumerate(stage_list):
            x_ap = x_in if i == 0 else xd[i - 1]
            y_ap = xo if i == 7 else xd[i]
            with ExitStack() as st:
                if kind == "attn":
                    cfg = cfgs[ckey]
                    ap = {"x": x_ap, "ident": g["ident"]}
                    ap["wqkv"] = din(pfx + "wqkv", [3 * MYF, D])
                    ap["bqkv"] = din(pfx + "bqkv", [1, 3 * MYF])
                    ap["wo"] = din(pfx + "wo", [D, MYF])
                    ap["bo"] = din(pfx + "bo", [1, D])
                    if cfg["causal"]:
                        for s_ in range(4):
                            ap[f"triq_{s_}"] = g[f"triq_{s_}"]
                            ap[f"trik_{s_}"] = g[f"trik_{s_}"]
                    if cfg["decay"]:
                        ap["poskq"] = g["poskq"]; ap["poskk"] = g["poskk"]
                        for s_ in range(4):
                            ap[f"trikq_{s_}"] = g[f"trikq_{s_}"]
                    if cfg["rope"]:
                        ap["ropec"] = g["ropec"]; ap["ropes"] = g["ropes"]
                    if cfg["topk"]:
                        ap["identf"] = g["identf"]
                    if cfg["relbias"]:
                        ap["bexr"] = din(pfx + "bexr", [MYH, LQT])
                        ap["bex"] = din(pfx + "bex", [MYH, LQT])
                        ap["bfar"] = din(pfx + "bfar", [1, 2 * MYH])
                    if cfg["gates"]:
                        ap["gates"] = din(pfx + "gates", [1, MYH])
                    attn_stage(nc, ch, st, cfg, ap, y_ap, pfx)
                elif kind == "ffn":
                    ap = {"x": x_ap, "ident": g["ident"],
                          "w": din("ffn_w", [D, D])}
                    ffn_stage(nc, ch, st, ap, y_ap, pfx)
                else:
                    ap = {"x": x_ap, "ident": g["ident"],
                          "wgs": din("moe_wgs", [9, 2048, D]),
                          "wus": din("moe_wus", [9, 2048, D]),
                          "wds": din("moe_wds", [9, D, 2048]),
                          "rw": din("moe_rw", [8, D]),
                          "rb": din("moe_rb", [1, 8])}
                    moe_stage(nc, ch, st, ap, y_ap, pfx)
        with nc.Block() as block:
            ch.flush(block)
    return nc



CFGS = {
    "l0a": dict(name="l0a", causal=True, rope=True, decay=False, relbias=False,
                topk=False, gates=False, qscale=0.125),
    "l0b": dict(name="l0b", causal=True, rope=False, decay=False, relbias=False,
                topk=False, gates=False, qscale=0.125),
    "l1a": dict(name="l1a", causal=False, rope=False, decay=False, relbias=True,
                topk=True, gates=True, qscale=0.125),
    "l1b": dict(name="l1b", causal=True, rope=False, decay=True, relbias=False,
                topk=False, gates=False, qscale=0.125),
}

_consts = make_consts()
_nc_cache = {}


def _get_nc(kind):
    if kind not in _nc_cache:
        if kind == "ffn":
            _nc_cache[kind] = build_ffn()
        elif kind == "moe":
            _nc_cache[kind] = build_moe()
        else:
            _nc_cache[kind] = build_attn(CFGS[kind])
    return _nc_cache[kind]


def attn_core_inputs(kind, c, x, wq, wk, wv, bq, bk, bv, wo, bo,
                     head_gates=None, rel_pos_bias=None):
    cfg = CFGS[kind]
    b, half = c // 2, c % 2
    hs = slice(512 * half, 512 * half + 512)
    m = {
        "x": np.ascontiguousarray(x[b, ROWS * half:ROWS * (half + 1)]),
        "wqkv": np.ascontiguousarray(np.concatenate([wq[hs], wk[hs], wv[hs]], 0)),
        "bqkv": np.ascontiguousarray(np.concatenate([bq[hs], bk[hs], bv[hs]])[None, :]),
        "wo": np.ascontiguousarray(wo[:, 512 * half:512 * half + 512]),
        "bo": np.ascontiguousarray(bo[None, :]),
        "ident": _consts["ident"],
    }
    if cfg["causal"]:
        for s_ in range(4):
            m[f"triq_{s_}"] = _consts[f"triq_{s_}"]
            m[f"trik_{s_}"] = _consts[f"trik_{s_}"]
    if cfg["decay"]:
        m["poskq"] = _consts["poskq"]
        m["poskk"] = _consts["poskk"]
        for d_ in range(4):
            m[f"trikq_{d_}"] = _consts[f"trikq_{d_}"]
    if cfg["rope"]:
        m["ropec"] = _consts["ropec"]
        m["ropes"] = _consts["ropes"]
    if cfg["topk"]:
        m["identf"] = _consts["identf"]
    if cfg["relbias"]:
        heads = range(MYH * half, MYH * half + MYH)
        bexr = np.zeros((MYH, LQT), dtype=np.float32)
        bex = np.zeros((MYH, LQT), dtype=np.float32)
        bfar = np.zeros((1, 2 * MYH), dtype=np.float32)
        idx = np.clip(TMAX - np.arange(LQT), -511, 511) + 511
        idx2 = np.clip(np.arange(LQT) - 1279, -511, 511) + 511
        for j, h in enumerate(heads):
            bexr[j] = rel_pos_bias[idx, h]
            bex[j] = rel_pos_bias[idx2, h]
            bfar[0, 2 * j] = rel_pos_bias[0, h]
            bfar[0, 2 * j + 1] = rel_pos_bias[1022, h]
        m["bexr"] = bexr
        m["bex"] = bex
        m["bfar"] = bfar
    if cfg["gates"]:
        m["gates"] = np.ascontiguousarray(head_gates[MYH * half:MYH * half + MYH][None, :])
    return m


def _scatter(x, res):
    xn = np.empty_like(x)
    for c in range(NCORE):
        b, half = c // 2, c % 2
        xn[b, ROWS * half:ROWS * (half + 1)] = res.results[c]["xo"]
    return xn


def _run_attn(kind, x, mkmap):
    maps = [mkmap(c) for c in range(NCORE)]
    res = run_bass_kernel_spmd(_get_nc(kind), maps, core_ids=list(range(NCORE)))
    return _scatter(x, res)


def _run_rowlocal(kind, x, extra):
    maps = []
    for c in range(NCORE):
        b, half = c // 2, c % 2
        m = {"x": np.ascontiguousarray(x[b, ROWS * half:ROWS * (half + 1)]),
             "ident": _consts["ident"]}
        m.update(extra)
        maps.append(m)
    res = run_bass_kernel_spmd(_get_nc(kind), maps, core_ids=list(range(NCORE)))
    return _scatter(x, res)


def kernel(L0_attn_a_X, L0_attn_a_W_KQV, L0_attn_a_W_O, L0_attn_b_W_qkv,
           L0_attn_b_b_qkv, L0_attn_b_W_o, L0_attn_b_b_o, L0_ffn_weight,
           L1_attn_a_Wq, L1_attn_a_bq, L1_attn_a_Wk, L1_attn_a_bk, L1_attn_a_Wv,
           L1_attn_a_bv, L1_attn_a_Wo, L1_attn_a_bo, L1_attn_a_head_gates,
           L1_attn_a_rel_pos_bias, L1_attn_b_Wq, L1_attn_b_bq, L1_attn_b_Wk,
           L1_attn_b_bk, L1_attn_b_Wv, L1_attn_b_bv, L1_attn_b_Wo, L1_attn_b_bo,
           L1_ffn_sg_w, L1_ffn_su_w, L1_ffn_sd_w, L1_ffn_router_w,
           L1_ffn_routing_bias, L1_ffn_wg, L1_ffn_wu, L1_ffn_wd):
    f32 = np.float32
    x = np.asarray(L0_attn_a_X, dtype=f32)
    z1 = np.zeros(D, f32)
    kqv = np.asarray(L0_attn_a_W_KQV, f32)
    wqkv_b = np.asarray(L0_attn_b_W_qkv, f32)
    bqkv_b = np.asarray(L0_attn_b_b_qkv, f32)
    attn_args = {
        "a0_": ("l0a", kqv[D:2 * D], kqv[0:D], kqv[2 * D:3 * D], z1, z1, z1,
                np.asarray(L0_attn_a_W_O, f32), z1, None, None),
        "b0_": ("l0b", wqkv_b[0:D], wqkv_b[D:2 * D], wqkv_b[2 * D:], bqkv_b[0:D],
                bqkv_b[D:2 * D], bqkv_b[2 * D:], np.asarray(L0_attn_b_W_o, f32),
                np.asarray(L0_attn_b_b_o, f32), None, None),
        "c0_": ("l1a", np.asarray(L1_attn_a_Wq, f32), np.asarray(L1_attn_a_Wk, f32),
                np.asarray(L1_attn_a_Wv, f32), np.asarray(L1_attn_a_bq, f32),
                np.asarray(L1_attn_a_bk, f32), np.asarray(L1_attn_a_bv, f32),
                np.asarray(L1_attn_a_Wo, f32), np.asarray(L1_attn_a_bo, f32),
                np.asarray(L1_attn_a_head_gates, f32),
                np.asarray(L1_attn_a_rel_pos_bias, f32)),
        "d0_": ("l1b", np.asarray(L1_attn_b_Wq, f32), np.asarray(L1_attn_b_Wk, f32),
                np.asarray(L1_attn_b_Wv, f32), np.asarray(L1_attn_b_bq, f32),
                np.asarray(L1_attn_b_bk, f32), np.asarray(L1_attn_b_bv, f32),
                np.asarray(L1_attn_b_Wo, f32), np.asarray(L1_attn_b_bo, f32),
                None, None),
    }
    if "fused" not in _nc_cache:
        _nc_cache["fused"] = build_fused(CFGS)
    nc = _nc_cache["fused"]
    maps = []
    for c in range(NCORE):
        b, half = c // 2, c % 2
        m = {"x": np.ascontiguousarray(x[b, ROWS * half:ROWS * (half + 1)]),
             "ident": _consts["ident"], "identf": _consts["identf"],
             "poskq": _consts["poskq"], "poskk": _consts["poskk"],
             "ropec": _consts["ropec"], "ropes": _consts["ropes"]}
        for s_ in range(4):
            m[f"triq_{s_}"] = _consts[f"triq_{s_}"]
            m[f"trik_{s_}"] = _consts[f"trik_{s_}"]
            m[f"trikq_{s_}"] = _consts[f"trikq_{s_}"]
        for pfx, args in attn_args.items():
            kind, wq, wk, wv, bq, bk, bv, wo, bo, hg, rpb = args
            am = attn_core_inputs(kind, c, x, wq, wk, wv, bq, bk, bv, wo, bo,
                                  head_gates=hg, rel_pos_bias=rpb)
            for k in ("wqkv", "bqkv", "wo", "bo", "bexr", "bex", "bfar", "gates"):
                if k in am:
                    m[pfx + k] = am[k]
        m["ffn_w"] = np.ascontiguousarray(np.asarray(L0_ffn_weight, f32))
        m["moe_wgs"] = np.ascontiguousarray(np.concatenate(
            [np.asarray(L1_ffn_wg, f32), np.asarray(L1_ffn_sg_w, f32)[None]], 0))
        m["moe_wus"] = np.ascontiguousarray(np.concatenate(
            [np.asarray(L1_ffn_wu, f32), np.asarray(L1_ffn_su_w, f32)[None]], 0))
        m["moe_wds"] = np.ascontiguousarray(np.concatenate(
            [np.asarray(L1_ffn_wd, f32), np.asarray(L1_ffn_sd_w, f32)[None]], 0))
        m["moe_rw"] = np.ascontiguousarray(np.asarray(L1_ffn_router_w, f32))
        m["moe_rb"] = np.ascontiguousarray(np.asarray(L1_ffn_routing_bias, f32)[None, :])
        maps.append(m)
    res = run_bass_kernel_spmd(nc, maps, core_ids=list(range(NCORE)))
    xn = np.empty_like(x)
    for c in range(NCORE):
        b, half = c // 2, c % 2
        xn[b, ROWS * half:ROWS * (half + 1)] = res.results[c]["xo"]
    return xn


# version 08 (final)
# speedup vs baseline: 1.5039x; 1.5039x over previous
import numpy as np
import sys
from contextlib import ExitStack
sys.path.insert(0, '/opt/trn_rl_repo')
import concourse.bass as bass
import concourse.mybir as mybir
from concourse.bass_utils import run_bass_kernel_spmd
from concourse.ap import AP as RawAP
import ml_dtypes

F32 = mybir.dt.float32
BF16 = mybir.dt.bfloat16
U8 = mybir.dt.uint8
AF = mybir.ActivationFunctionType
ALU = mybir.AluOpType
AX = mybir.AxisListType
BF = ml_dtypes.bfloat16

D = 1024
NH = 16
DH = 64
S = 4096
B = 4
NCORE = 8
ROWS = 2048


class Ch:
    ENGS = ("sync", "gp", "v", "s", "t")

    def __init__(self, nc):
        self.nc = nc
        self.sem = {}
        self.cnt = {e: 0 for e in self.ENGS}
        self.last = None
        self.prev_small = {e: False for e in self.ENGS}
        self.q = {e: [] for e in self.ENGS}

    def open_sems(self, stack):
        for e in self.ENGS:
            self.sem[e] = stack.enter_context(self.nc.semaphore("chs_" + e))

    def tok(self, e=None):
        if e is None:
            e = self.last[0]
        return (e, self.cnt[e])

    def run(self, e, fn, wait=None, small=False, dma=False, fence=False, chain=True):
        waits = []
        if wait is not None:
            waits = [w for w in wait if w[1] > 0]
        elif chain and self.last is not None:
            le, lc = self.last
            if le != e:
                waits = [(le, lc)]
            elif fence or self.prev_small[e]:
                waits = [(e, lc)]
        elif fence and self.last is not None and self.last[0] == e:
            waits = [(e, self.last[1])]
        step = 16 if dma else 1
        self.q[e].append((waits, fn, step))
        self.cnt[e] += step
        t = (e, self.cnt[e])
        if chain:
            self.last = t
        self.prev_small[e] = small
        return t

    def join(self, toks, eng="gp"):
        toks = [t for t in toks if t is not None and t[1] > 0]
        m = {}
        for (e, c) in toks:
            m[e] = max(m.get(e, 0), c)
        self.run(eng, lambda E: E.nop(), wait=[(e, c) for e, c in m.items()])
        for e in self.ENGS:
            self.prev_small[e] = True

    def flush(self, block):
        def replay(ename):
            def f(eng):
                sem = self.sem
                for (waits, fn, step) in self.q[ename]:
                    for (we, wc) in waits:
                        eng.wait_ge(sem[we], wc)
                    instr = fn(eng)
                    instr.then_inc(sem[ename], step)
            return f
        block.sync(replay("sync"))
        block.gpsimd(replay("gp"))
        block.vector(replay("v"))
        block.scalar(replay("s"))
        block.tensor(replay("t"))


def make_consts():
    c = {}
    c["ident"] = np.eye(128, dtype=BF)
    c["identf"] = np.eye(128, dtype=np.float32)
    big = np.float32(1e30)
    q = np.arange(128)[:, None]
    k = np.arange(512)[None, :]
    for s_ in range(4):
        c[f"triq_{s_}"] = np.where(k > q + 128 * s_, -big, np.float32(0)).astype(np.float32)
    ki = np.arange(128)[:, None]
    qj = np.arange(512)[None, :]
    for d_ in range(4):
        c[f"trik_{d_}"] = (ki + 128 * d_ <= qj).astype(BF)
        c[f"trikq_{d_}"] = np.where(ki + 128 * d_ > qj, -big, np.float32(0)).astype(np.float32)
    c["poskq"] = np.broadcast_to(np.arange(512, dtype=np.float32)[None, :], (128, 512)).copy()
    c["poskk"] = (np.arange(128)[:, None] + 128 * np.arange(32)[None, :]).astype(np.float32)
    inv = 1.0 / (10000.0 ** (np.arange(0, DH, 2, dtype=np.float32) / DH))
    pos = np.arange(S, dtype=np.float32)
    fr = pos[:, None] * inv[None, :]
    c["ropec"] = np.cos(fr).astype(np.float32)
    c["ropes"] = np.sin(fr).astype(np.float32)
    return c



EPS = 1e-5
MYH = 8
MYF = MYH * DH
LQW = 2304
LQT = 2431
TMAX = 1151
W2K = 2172
NITER_TOPK = 12


class NS(dict):
    __getattr__ = dict.__getitem__
    __setattr__ = dict.__setitem__



def ln_tile(ch, n, x_ap, xn_ap):
    sm, dump = n.sm, n.dump
    ch.run("v", lambda e: e.reduce_sum(out=sm[:, 0:1], in_=x_ap, axis=AX.X), small=True)
    ch.run("s", lambda e: e.activation(out=dump[:], in_=x_ap, func=AF.Square,
                                       bias=0.0, scale=1.0, accum_out=sm[:, 1:2]), small=True)
    ch.run("v", lambda e: e.tensor_scalar(out=sm[:, 2:3], in0=sm[:, 0:1], scalar1=1.0 / D,
                                          scalar2=None, op0=ALU.mult), small=True)
    ch.run("v", lambda e: e.tensor_scalar(out=sm[:, 3:4], in0=sm[:, 1:2], scalar1=1.0 / D,
                                          scalar2=None, op0=ALU.mult), small=True)
    ch.run("v", lambda e: e.tensor_tensor(out=sm[:, 4:5], in0=sm[:, 2:3], in1=sm[:, 2:3],
                                          op=ALU.mult), small=True)
    ch.run("v", lambda e: e.tensor_tensor(out=sm[:, 5:6], in0=sm[:, 3:4], in1=sm[:, 4:5],
                                          op=ALU.subtract), small=True)
    ch.run("s", lambda e: e.activation(out=sm[:, 6:7], in_=sm[:, 5:6], func=AF.Sqrt,
                                       bias=n.epsv[:], scale=1.0), small=True)
    ch.run("v", lambda e: e.reciprocal(out=sm[:, 7:8], in_=sm[:, 6:7]), small=True)
    ch.run("v", lambda e: e.tensor_scalar(out=xn_ap, in0=x_ap, scalar1=sm[:, 2:3],
                                          scalar2=sm[:, 7:8], op0=ALU.subtract, op1=ALU.mult))


def transpose_128(ch, n, out_ap_fn, in_ap_fn, nch):
    for k in range(nch):
        ch.run("t", lambda e, k=k: e.transpose(out=n.pt_bf[:, 128 * k:128 * (k + 1)],
                                               in_=in_ap_fn(k), identity=n.ident[:]))
    for k in range(nch):
        ch.run("v", lambda e, k=k: e.tensor_copy(out=out_ap_fn(k),
                                                 in_=n.pt_bf[:, 128 * k:128 * (k + 1)]))


def load_wT(ch, n, src_ap, wT, rows, cols=1024, sign=False):
    for j in range(rows // 128):
        ch.run("sync", lambda e, j=j, cols=cols: e.dma_start(out=n.wtmp_f[:, 0:cols], in_=src_ap[128 * j:128 * (j + 1), :]), dma=True)
        if sign:
            ch.run("s", lambda e, cols=cols: e.sign(out=n.wtmp_b[:, 0:cols], in_=n.wtmp_f[:, 0:cols]))
        else:
            ch.run("v", lambda e, cols=cols: e.tensor_copy(out=n.wtmp_b[:, 0:cols], in_=n.wtmp_f[:, 0:cols]))
        transpose_128(ch, n, lambda k, j=j: wT[k][:, 128 * j:128 * (j + 1)],
                      lambda k: n.wtmp_b[:, 128 * k:128 * (k + 1)], cols // 128)


def bcast_row(ch, n, out_sb, row_ap, ncols):
    for j in range(0, ncols, 512):
        w = min(512, ncols - j)
        ch.run("t", lambda e, j=j, w=w: e.matmul(out=n.pbc[:, 0:w], lhsT=n.onescol_f[0:1, :],
                                                 rhs=row_ap[0:1, j:j + w], start=True, stop=True))
        ch.run("v", lambda e, j=j, w=w: e.tensor_copy(out=out_sb[:, j:j + w], in_=n.pbc[:, 0:w]), small=(w < 256))



def build_attn(cfg):
    nc = bass.Bass("TRN2", target_bir_lowering=False)
    ap = {}
    def din(name, shape, dt=F32):
        ap[name] = nc.dram_tensor(name, shape, dt, kind="ExternalInput").ap()
    din("x", [ROWS, D])
    din("wqkv", [3 * MYF, D])
    din("bqkv", [1, 3 * MYF])
    din("wo", [D, MYF])
    din("bo", [1, D])
    din("ident", [128, 128], BF16)
    if cfg["causal"]:
        for s_ in range(4):
            din(f"triq_{s_}", [128, 512])
            din(f"trik_{s_}", [128, 512], BF16)
    if cfg["decay"]:
        din("poskq", [128, 512])
        din("poskk", [128, 32])
        for d_ in range(4):
            din(f"trikq_{d_}", [128, 512])
    if cfg["rope"]:
        din("ropec", [S, DH // 2])
        din("ropes", [S, DH // 2])
    if cfg["topk"]:
        din("identf", [128, 128])
    if cfg["relbias"]:
        din("bexr", [MYH, LQT])
        din("bex", [MYH, LQT])
        din("bfar", [1, 2 * MYH])
    if cfg["gates"]:
        din("gates", [1, MYH])
    xo = nc.dram_tensor("xo", [ROWS, D], F32, kind="ExternalOutput").ap()

    st = ExitStack()
    with st:
        ch = Ch(nc)
        ch.open_sems(st)
        attn_stage(nc, ch, st, cfg, ap, xo, "")
        with nc.Block() as block:
            ch.flush(block)
    return nc


def attn_stage(nc, ch, stage_st, cfg, ap, xo, pfx):
    if cfg.get("dbg"):
        ap["osb_o"] = nc.dram_tensor(pfx + "osb_o", [128, 32 * MYF], BF16, kind="ExternalOutput").ap()
        ap["dbg_o"] = nc.dram_tensor(pfx + "dbg_o", [128, 64], F32, kind="ExternalOutput").ap()
    dr = NS()
    dr.xn_d = nc.dram_tensor(pfx + "xn_d", [ROWS, D], BF16).ap()
    dr.xg_d = nc.dram_tensor(pfx + "xg_d", [S, D], BF16).ap()
    dr.qf_d = nc.dram_tensor(pfx + "qf_d", [MYF, S], BF16).ap()
    dr.kf_d = nc.dram_tensor(pfx + "kf_d", [MYF, S], BF16).ap()
    dr.vt_d = nc.dram_tensor(pfx + "vt_d", [S, MYF], BF16).ap()
    if cfg["topk"]:
        dr.qflo_d = nc.dram_tensor(pfx + "qflo_d", [MYF, S], BF16).ap()
        dr.kflo_d = nc.dram_tensor(pfx + "kflo_d", [MYF, S], BF16).ap()
        dr.xnlo_d = nc.dram_tensor(pfx + "xnlo_d", [ROWS, D], BF16).ap()
        dr.xglo_d = nc.dram_tensor(pfx + "xglo_d", [S, D], BF16).ap()
    dr.y_d = nc.dram_tensor(pfx + "y_d", [S, D], F32).ap()
    dr.yr_d = nc.dram_tensor(pfx + "yr_d", [ROWS, D], F32).ap()
    st = stage_st
    if True:
        n = NS()
        def sb(name, shape, dt):
            t = st.enter_context(nc.sbuf_tensor(pfx + "sb_" + name, shape, dt))
            n[name] = t
            return t
        sb("ident", [128, 128], BF16)
        n.wqkvT = [st.enter_context(nc.sbuf_tensor(pfx + f"wqkvT{k}", [128, 3 * MYF], BF16)) for k in range(8)]
        n.woT = [st.enter_context(nc.sbuf_tensor(pfx + f"woT{k}", [128, D], BF16)) for k in range(4)]
        sb("wtmp_f", [128, D], F32); sb("wtmp_b", [128, D], BF16)
        sb("x_t", [128, D], F32); sb("xb_t", [128, D], F32)
        sb("dump", [128, D], F32); sb("sm", [128, 16], F32); sb("epsv", [128, 1], F32)
        sb("xn_t", [128, D], BF16); sb("xn_f", [128, D], BF16)
        sb("qkv_t", [128, 3 * MYF], F32); sb("qkv_b", [128, 2 * MYF], BF16)
        sb("bqkv_bc", [128, 3 * MYF], F32); sb("bo_bc", [128, D], F32)
        sb("bqkv_r", [1, 3 * MYF], F32); sb("bo_r", [1, D], F32)
        sb("onescol_f", [1, 128], F32); sb("onescol_b", [128, 1], BF16)
        sb("onesrow_b", [1, 512], BF16)
        if cfg["causal"]:
            n.triq = [st.enter_context(nc.sbuf_tensor(pfx + f"triqs{i}", [128, 512], F32)) for i in range(4)]
            n.trik = [st.enter_context(nc.sbuf_tensor(pfx + f"triks{i}", [128, 512], BF16)) for i in range(4)]
        if cfg["rope"]:
            sb("rc_t", [128, DH // 2], F32); sb("rs_t", [128, DH // 2], F32)
            sb("rtmp", [128, MYF], F32)
        if cfg["decay"]:
            sb("poskq", [128, 512], F32); sb("poskk", [128, 32], F32)
            sb("kcoff", [128, 8], F32)
            n.trikq = [st.enter_context(nc.sbuf_tensor(pfx + f"trikqs{i}", [128, 512], F32)) for i in range(4)]
        sb("kh", [66, S], BF16); sb("qa", [66, 512], BF16)
        sb("stmp", [128, 512], F32); sb("cm", [128, 32], F32)
        n.stmpr = [n.stmp, st.enter_context(nc.sbuf_tensor(pfx + "sb_stmp1", [128, 512], F32))]
        sb("Mv", [128, 1], F32); sb("mrow_b", [128, 1], BF16)
        sb("PT", [128, 512], BF16); sb("PT2", [128, 512], BF16)
        n.PTr = [n.PT, st.enter_context(nc.sbuf_tensor(pfx + "sb_PTb", [128, 512], BF16))]
        n.PT2r = [n.PT2, st.enter_context(nc.sbuf_tensor(pfx + "sb_PT2b", [128, 512], BF16))]
        sb("vc", [128, MYF], BF16)
        n.vcr = [n.vc, st.enter_context(nc.sbuf_tensor(pfx + "sb_vcb", [128, MYF], BF16))]
        sb("osb", [128, 32 * MYF], BF16)
        sb("rz", [128, 4], F32)
        sb("off_t", [128, D], BF16)
        if cfg["topk"]:
            sb("khlo", [66, S], BF16); sb("qalo", [66, 512], BF16)
            sb("qkvlo_b", [128, 2 * MYF], BF16)
            sb("identf", [128, 128], F32)
            sb("xn32", [128, D], F32)
            sb("xnlo_t", [128, D], BF16); sb("xnlo_f", [128, D], BF16)
            sb("Mb32", [128, 1], F32)
            sb("taur_f", [1, 512], F32)
            n.wqkvTlo = [st.enter_context(nc.sbuf_tensor(pfx + f"wqkvTlo{k}", [128, 2 * MYF], BF16)) for k in range(8)]
            sb("ssb", [128, S], F32); sb("segm", [128, 64], F32)
            sb("lov", [128, 1], F32); sb("hiv", [128, 1], F32)
            sb("midv", [128, 1], F32); sb("cntv", [128, 1], F32)
            sb("mskv", [128, 1], U8); sb("tjunk", [128, S // 2], BF16)
            sb("taub", [128, 512], F32)
        if cfg["relbias"]:
            sb("skq", [128, LQW], BF16); sb("skk", [128, W2K], BF16)
            sb("bfar_r", [1, 2 * MYH], F32)
            sb("farv", [128, 2], F32)
        if cfg["gates"]:
            sb("gate_r", [1, MYH], F32); sb("gate_bc", [128, MYH], F32)
        n.pb = [st.enter_context(nc.psum_tensor(pfx + f"pb{i}", [128, 512], F32)) for i in range(6)]
        n.pbc = st.enter_context(nc.psum_tensor(pfx + "pbc", [128, 512], F32))
        n.pt_bf = st.enter_context(nc.psum_tensor(pfx + "pt_bf", [128, 1024], BF16))
        n.ap = ap
        n.dr = dr
        n.xo = xo
        emit_attn(ch, cfg, n)


def emit_attn(ch, cfg, n):
    ap, dr = n.ap, n.dr
    pb = n.pb

    def coll(kind, op, in_ap, out_ap):
        ch.run("gp", lambda e: e.collective_compute(
            kind, op, replica_groups=[[0, 1], [2, 3], [4, 5], [6, 7]],
            ins=[in_ap.opt()], outs=[out_ap.opt()]))

    ch.run("v", lambda e: e.memset(n.onescol_f[:], 1.0))
    ch.run("v", lambda e: e.memset(n.epsv[:], float(EPS)), small=True)
    ch.run("v", lambda e: e.memset(n.onescol_b[:], 1.0))
    ch.run("v", lambda e: e.memset(n.onesrow_b[:], 1.0))
    ch.run("v", lambda e: e.memset(n.kh[64:66, :], 0.0))
    ch.run("v", lambda e: e.memset(n.kh[64:65, :], 1.0), fence=True)
    ch.run("v", lambda e: e.memset(n.qa[64:66, :], 0.0))
    if cfg["topk"]:
        ch.run("v", lambda e: e.memset(n.khlo[64:66, :], 0.0))
        ch.run("v", lambda e: e.memset(n.qalo[64:66, :], 0.0))
    ch.run("sync", lambda e: e.dma_start(out=n.ident[:], in_=ap["ident"]), dma=True)
    if cfg["causal"]:
        for s_ in range(4):
            ch.run("sync", lambda e, s_=s_: e.dma_start(out=n.triq[s_][:], in_=ap[f"triq_{s_}"]), dma=True)
            ch.run("sync", lambda e, s_=s_: e.dma_start(out=n.trik[s_][:], in_=ap[f"trik_{s_}"]), dma=True)
    ch.run("sync", lambda e: e.dma_start(out=n.bqkv_r[:], in_=ap["bqkv"]), dma=True)
    ch.run("sync", lambda e: e.dma_start(out=n.bo_r[:], in_=ap["bo"]), dma=True)
    if cfg["decay"]:
        ch.run("sync", lambda e: e.dma_start(out=n.poskq[:], in_=ap["poskq"]), dma=True)
        ch.run("sync", lambda e: e.dma_start(out=n.poskk[:], in_=ap["poskk"]), dma=True)
        for d_ in range(4):
            ch.run("sync", lambda e, d_=d_: e.dma_start(out=n.trikq[d_][:], in_=ap[f"trikq_{d_}"]), dma=True)
        for c in range(8):
            ch.run("v", lambda e, c=c: e.memset(n.kcoff[:, c:c + 1], float(512 * c)), small=True)
    if cfg["relbias"]:
        ch.run("sync", lambda e: e.dma_start(out=n.bfar_r[:], in_=ap["bfar"]), dma=True)
    if cfg["gates"]:
        ch.run("sync", lambda e: e.dma_start(out=n.gate_r[:], in_=ap["gates"]), dma=True)
        bcast_row(ch, n, n.gate_bc, n.gate_r[:], MYH)
    bcast_row(ch, n, n.bqkv_bc, n.bqkv_r[:], 3 * MYF)
    bcast_row(ch, n, n.bo_bc, n.bo_r[:], D)
    qsc = cfg["qscale"]
    if qsc != 1.0:
        ch.run("v", lambda e: e.tensor_scalar(out=n.bqkv_bc[:, 0:MYF], in0=n.bqkv_bc[:, 0:MYF],
                                              scalar1=qsc, scalar2=None, op0=ALU.mult), fence=True)
    if cfg["topk"]:
        ch.run("sync", lambda e: e.dma_start(out=n.identf[:], in_=ap["identf"]), dma=True)
    load_wT(ch, n, ap["wqkv"], n.wqkvT, 3 * MYF)
    if cfg["topk"]:
        for j in range(2 * MYF // 128):
            ch.run("sync", lambda e, j=j: e.dma_start(out=n.wtmp_f[:], in_=ap["wqkv"][128 * j:128 * (j + 1), :]), dma=True)
            ch.run("v", lambda e: e.tensor_copy(out=n.wtmp_b[:], in_=n.wtmp_f[:]))
            ch.run("v", lambda e: e.tensor_tensor(out=n.dump[:], in0=n.wtmp_f[:], in1=n.wtmp_b[:], op=ALU.subtract))
            ch.run("v", lambda e: e.tensor_copy(out=n.wtmp_b[:], in_=n.dump[:]), fence=True)
            transpose_128(ch, n, lambda k, j=j: n.wqkvTlo[k][:, 128 * j:128 * (j + 1)],
                          lambda k: n.wtmp_b[:, 128 * k:128 * (k + 1)], 8)
    load_wT(ch, n, ap["wo"], n.woT, D, cols=MYF)

    for t in range(ROWS // 128):
        ch.run("sync", lambda e, t=t: e.dma_start(out=n.x_t[:], in_=ap["x"][128 * t:128 * (t + 1), :]), dma=True)
        if cfg["topk"]:
            ln_tile(ch, n, n.x_t[:], n.xn32[:])
            ch.run("v", lambda e: e.tensor_copy(out=n.xn_t[:], in_=n.xn32[:]))
            ch.run("v", lambda e: e.tensor_tensor(out=n.dump[:], in0=n.xn32[:], in1=n.xn_t[:], op=ALU.subtract))
            ch.run("v", lambda e: e.tensor_copy(out=n.xnlo_t[:], in_=n.dump[:]))
            ch.run("sync", lambda e, t=t: e.dma_start(out=dr.xnlo_d[128 * t:128 * (t + 1), :], in_=n.xnlo_t[:]), dma=True)
        else:
            ln_tile(ch, n, n.x_t[:], n.xn_t[:])
        ch.run("sync", lambda e, t=t: e.dma_start(out=dr.xn_d[128 * t:128 * (t + 1), :], in_=n.xn_t[:]), dma=True)

    coll("AllGather", ALU.bypass, dr.xn_d, dr.xg_d)
    if cfg["topk"]:
        coll("AllGather", ALU.bypass, dr.xnlo_d, dr.xglo_d)

    for t in range(S // 128):
        ch.run("sync", lambda e, t=t: e.dma_start(out=n.xn_t[:], in_=dr.xg_d[128 * t:128 * (t + 1), :]), dma=True)
        transpose_128(ch, n, lambda k: n.xn_f[:, 128 * k:128 * (k + 1)],
                      lambda k: n.xn_t[:, 128 * k:128 * (k + 1)], 8)
        if cfg["topk"]:
            ch.run("sync", lambda e, t=t: e.dma_start(out=n.xnlo_t[:], in_=dr.xglo_d[128 * t:128 * (t + 1), :]), dma=True)
            transpose_128(ch, n, lambda k: n.xnlo_f[:, 128 * k:128 * (k + 1)],
                          lambda k: n.xnlo_t[:, 128 * k:128 * (k + 1)], 8)
        for og in range(3):
            bank = pb[og % 2]
            three = cfg["topk"] and og < 2
            for k in range(8):
                ch.run("t", lambda e, og=og, k=k, bank=bank: e.matmul(
                    out=bank[:], lhsT=n.xn_f[:, 128 * k:128 * (k + 1)],
                    rhs=n.wqkvT[k][:, 512 * og:512 * (og + 1)],
                    start=(k == 0), stop=(k == 7 and not three)))
            if three:
                for k in range(8):
                    ch.run("t", lambda e, og=og, k=k, bank=bank: e.matmul(
                        out=bank[:], lhsT=n.xnlo_f[:, 128 * k:128 * (k + 1)],
                        rhs=n.wqkvT[k][:, 512 * og:512 * (og + 1)],
                        start=False, stop=False))
                for k in range(8):
                    ch.run("t", lambda e, og=og, k=k, bank=bank: e.matmul(
                        out=bank[:], lhsT=n.xn_f[:, 128 * k:128 * (k + 1)],
                        rhs=n.wqkvTlo[k][:, 512 * og:512 * (og + 1)],
                        start=False, stop=(k == 7)))
            sc = qsc if og == 0 else 1.0
            ch.run("v", lambda e, og=og, bank=bank, sc=sc: e.scalar_tensor_tensor(
                out=n.qkv_t[:, 512 * og:512 * (og + 1)], in0=bank[:], scalar=sc,
                in1=n.bqkv_bc[:, 512 * og:512 * (og + 1)], op0=ALU.mult, op1=ALU.add))
        if cfg["rope"]:
            rope_pair(ch, cfg, n, t)
        ch.run("v", lambda e: e.tensor_copy(out=n.qkv_b[:], in_=n.qkv_t[:, 0:2 * MYF]))
        ch.run("v", lambda e: e.tensor_copy(out=n.vc[:], in_=n.qkv_t[:, 2 * MYF:3 * MYF]))
        ch.run("sync", lambda e, t=t: e.dma_start(out=dr.vt_d[128 * t:128 * (t + 1), :], in_=n.vc[:]), dma=True)
        dsts = [(0, n.qkv_b, dr.qf_d), (1, n.qkv_b, dr.kf_d)]
        if cfg["topk"]:
            ch.run("v", lambda e: e.tensor_tensor(out=n.dump[:], in0=n.qkv_t[:, 0:2 * MYF],
                                                  in1=n.qkv_b[:], op=ALU.subtract))
            ch.run("v", lambda e: e.tensor_copy(out=n.qkvlo_b[:], in_=n.dump[:]))
            dsts += [(0, n.qkvlo_b, dr.qflo_d), (1, n.qkvlo_b, dr.kflo_d)]
        for part, srcb, dst in dsts:
            transpose_128(ch, n, lambda k: n.xn_f[:, 128 * k:128 * (k + 1)],
                          lambda k, part=part, srcb=srcb: srcb[:, 512 * part + 128 * k:512 * part + 128 * (k + 1)], 4)
            xf = n.xn_f[:]
            ch.run("sync", lambda e, t=t, dst=dst, xf=xf: e.dma_start(
                out=RawAP(dst.tensor, 128 * t, [[S, 128], [128 * S, 4], [1, 128]]),
                in_=RawAP(xf.tensor, xf.offset, [[xf.ap[0][0], 128], [128, 4], [1, 128]])), dma=True)

    for h in range(MYH):
        ch.run("sync", lambda e, h=h: e.dma_start(out=n.kh[0:64, :], in_=dr.kf_d[64 * h:64 * (h + 1), :]), dma=True)
        if cfg["topk"]:
            ch.run("sync", lambda e, h=h: e.dma_start(out=n.khlo[0:64, :], in_=dr.kflo_d[64 * h:64 * (h + 1), :]), dma=True)
        if cfg["relbias"]:
            ch.run("gp", lambda e, h=h: e.dma_start(
                out=n.skq[:], in_=RawAP(ap["bexr"].tensor, h * LQT, [[1, 128], [1, LQW]])), dma=True)
            ch.run("gp", lambda e, h=h: e.dma_start(
                out=n.skk[:], in_=RawAP(ap["bex"].tensor, h * LQT + 130, [[1, 128], [1, W2K]])), dma=True)
            ch.run("t", lambda e, h=h: e.matmul(out=n.pbc[:, 0:2], lhsT=n.onescol_f[0:1, :],
                                                rhs=n.bfar_r[0:1, 2 * h:2 * h + 2], start=True, stop=True))
            ch.run("v", lambda e: e.tensor_copy(out=n.farv[:], in_=n.pbc[:, 0:2]), small=True)
        for qb in range(8):
            q0 = 512 * qb
            ch.run("sync", lambda e, h=h, q0=q0: e.dma_start(
                out=n.qa[0:64, :], in_=dr.qf_d[64 * h:64 * (h + 1), q0:q0 + 512]), dma=True)
            if cfg["topk"]:
                ch.run("sync", lambda e, h=h, q0=q0: e.dma_start(
                    out=n.qalo[0:64, :], in_=dr.qflo_d[64 * h:64 * (h + 1), q0:q0 + 512]), dma=True)
            emit_head_block(ch, cfg, n, h, qb)

    for qq in range(32):
        transpose_128(ch, n, lambda k: n.off_t[:, 128 * k:128 * (k + 1)],
                      lambda k, qq=qq: n.osb[:, 512 * qq + 128 * k:512 * qq + 128 * (k + 1)], 4)
        for og in range(2):
            bank = pb[og % 2]
            for k in range(4):
                ch.run("t", lambda e, og=og, k=k, bank=bank: e.matmul(
                    out=bank[:], lhsT=n.off_t[:, 128 * k:128 * (k + 1)],
                    rhs=n.woT[k][:, 512 * og:512 * (og + 1)], start=(k == 0), stop=(k == 3)))
            ch.run("v", lambda e, og=og, bank=bank: e.tensor_copy(
                out=n.x_t[:, 512 * og:512 * (og + 1)], in_=bank[:]))
        ch.run("sync", lambda e, qq=qq: e.dma_start(out=dr.y_d[128 * qq:128 * (qq + 1), :], in_=n.x_t[:]), dma=True)

    coll("ReduceScatter", ALU.add, dr.y_d, dr.yr_d)

    for t in range(ROWS // 128):
        ch.run("sync", lambda e, t=t: e.dma_start(out=n.x_t[:], in_=ap["x"][128 * t:128 * (t + 1), :]), dma=True)
        ch.run("sync", lambda e, t=t: e.dma_start(out=n.xb_t[:], in_=dr.yr_d[128 * t:128 * (t + 1), :]), dma=True)
        ch.run("v", lambda e: e.tensor_tensor(out=n.dump[:], in0=n.x_t[:], in1=n.bo_bc[:], op=ALU.add))
        ch.run("v", lambda e: e.tensor_tensor(out=n.x_t[:], in0=n.dump[:], in1=n.xb_t[:], op=ALU.add))
        ch.run("sync", lambda e, t=t: e.dma_start(out=n.xo[128 * t:128 * (t + 1), :], in_=n.x_t[:]), dma=True)
    if cfg.get("dbg"):
        ch.run("sync", lambda e: e.dma_start(out=ap["osb_o"], in_=n.osb[:]), dma=True)
        ch.run("v", lambda e: e.tensor_copy(out=n.dump[:, 0:16], in_=n.cm[:, 0:16]))
        ch.run("v", lambda e: e.tensor_copy(out=n.dump[:, 16:17], in_=n.Mv[:]))
        ch.run("v", lambda e: e.tensor_copy(out=n.dump[:, 17:21], in_=n.rz[:]))
        if cfg["topk"]:
            ch.run("v", lambda e: e.tensor_copy(out=n.dump[:, 21:22], in_=n.lov[:]))
            ch.run("v", lambda e: e.tensor_copy(out=n.dump[:, 22:23], in_=n.cntv[:]))
            ch.run("v", lambda e: e.tensor_copy(out=n.dump[:, 23:24], in_=n.Mv[:]))
            ch.run("v", lambda e: e.tensor_copy(out=n.dump[:, 24:25], in_=n.hiv[:]))
            ch.run("v", lambda e: e.tensor_copy(out=n.dump[:, 25:26], in_=n.taub[:, 400:401]))
        ch.run("sync", lambda e: e.dma_start(out=ap["dbg_o"], in_=n.dump[:, 0:64]), dma=True)
    ch.run("gp", lambda e: e.memset(n.sm[:, 8:9], 0.0))


def rope_pair(ch, cfg, n, t):
    ap = n.ap
    ch.run("sync", lambda e, t=t: e.dma_start(out=n.rc_t[:], in_=ap["ropec"][128 * t:128 * (t + 1), :]), dma=True)
    ch.run("sync", lambda e, t=t: e.dma_start(out=n.rs_t[:], in_=ap["ropes"][128 * t:128 * (t + 1), :]), dma=True)
    cosb = n.rc_t[:].unsqueeze(1).broadcast_to([128, MYH, 32])
    sinb = n.rs_t[:].unsqueeze(1).broadcast_to([128, MYH, 32])
    for part in range(2):
        base = 512 * part
        xe = n.qkv_t[:, base:base + 512:2].rearrange("p (a b) -> p a b", a=MYH)
        xo_ = n.qkv_t[:, base + 1:base + 512:2].rearrange("p (a b) -> p a b", a=MYH)
        te = n.rtmp[:, 0:256].rearrange("p (a b) -> p a b", a=MYH)
        to = n.rtmp[:, 256:512].rearrange("p (a b) -> p a b", a=MYH)
        ch.run("v", lambda e, xe=xe, cosb=cosb, te=te: e.tensor_tensor(out=te, in0=xe, in1=cosb, op=ALU.mult))
        ch.run("v", lambda e, xo_=xo_, sinb=sinb, to=to: e.tensor_tensor(out=to, in0=xo_, in1=sinb, op=ALU.mult))
        ch.run("v", lambda e, te=te, to=to: e.tensor_tensor(out=te, in0=te, in1=to, op=ALU.subtract), fence=True)
        ch.run("v", lambda e, xe=xe, sinb=sinb, to=to: e.tensor_tensor(out=to, in0=xe, in1=sinb, op=ALU.mult))
        ch.run("v", lambda e, xo_=xo_, cosb=cosb: e.tensor_tensor(out=xo_, in0=xo_, in1=cosb, op=ALU.mult), fence=True)
        ch.run("v", lambda e, xo_=xo_, to=to: e.tensor_tensor(out=xo_, in0=to, in1=xo_, op=ALU.add), fence=True)
        ch.run("v", lambda e, xe=xe, te=te: e.tensor_copy(out=xe, in_=te))


def put_neg_m_row(ch, n, s_):
    ch.run("v", lambda e: e.tensor_scalar(out=n.mrow_b[:], in0=n.Mv[:], scalar1=-1.0,
                                          scalar2=None, op0=ALU.mult), small=True)
    ch.run("t", lambda e: e.transpose(out=n.pt_bf[0:1, 0:128], in_=n.mrow_b[:], identity=n.ident[:]))
    ch.run("v", lambda e, s_=s_: e.tensor_copy(out=n.qa[64:65, 128 * s_:128 * (s_ + 1)],
                                               in_=n.pt_bf[0:1, 0:128]), small=True)


def emit_head_block(ch, cfg, n, h, qb):
    pb = n.pb
    causal = cfg["causal"]
    q0 = 512 * qb
    R = lambda e, f, **kw: ch.run(e, f, chain=False, **kw)

    if cfg["topk"]:
        for s_ in range(4):
            emit_l1a_stats(ch, cfg, n, h, qb, s_)
    else:
        jobs = []
        for s_ in range(4):
            qa0 = q0 + 128 * s_
            nk = (qa0 + 128 + 511) // 512
            for kc in range(nk):
                jobs.append((s_, kc, nk))
        vt = {}
        mmt = {}
        base_t = ch.tok("t") if ch.cnt["t"] else None
        for i, (s_, kc, nk) in enumerate(jobs):
            qa0 = q0 + 128 * s_
            bank = pb[i % 2]
            w = [vt[i - 2]] if i >= 2 else ([ch.last] if ch.last else [])
            mmt[i] = R("t", lambda e, kc=kc, s_=s_, bank=bank: e.matmul(
                out=bank[:], lhsT=n.qa[0:64, 128 * s_:128 * (s_ + 1)],
                rhs=n.kh[0:64, 512 * kc:512 * kc + 512], start=True, stop=True), wait=w)
            isdiag = causal and (512 * kc <= qa0 < 512 * (kc + 1))
            tri = n.triq[(qa0 - 512 * kc) // 128] if isdiag else None
            if cfg["decay"]:
                if isdiag:
                    R("v", lambda e, bank=bank, tri=tri: e.scalar_tensor_tensor(
                        out=n.stmp[:], in0=bank[:], scalar=1.0, in1=tri[:], op0=ALU.mult, op1=ALU.add),
                      wait=[mmt[i]])
                    R("v", lambda e: e.tensor_tensor(out=n.stmp[:], in0=n.stmp[:],
                                                     in1=n.poskq[:], op=ALU.add), wait=[ch.tok("v")])
                else:
                    R("v", lambda e, bank=bank: e.scalar_tensor_tensor(
                        out=n.stmp[:], in0=bank[:], scalar=1.0, in1=n.poskq[:], op0=ALU.mult, op1=ALU.add),
                      wait=[mmt[i]])
                vt[i] = R("v", lambda e, s_=s_, kc=kc: e.reduce_max(
                    out=n.cm[:, 8 * s_ + kc:8 * s_ + kc + 1], in_=n.stmp[:], axis=AX.X), wait=[])
            elif isdiag:
                R("v", lambda e, bank=bank, tri=tri: e.scalar_tensor_tensor(
                    out=n.stmp[:], in0=bank[:], scalar=1.0, in1=tri[:], op0=ALU.mult, op1=ALU.add),
                  wait=[mmt[i]])
                vt[i] = R("v", lambda e, s_=s_, kc=kc: e.reduce_max(
                    out=n.cm[:, 8 * s_ + kc:8 * s_ + kc + 1], in_=n.stmp[:], axis=AX.X), wait=[])
            else:
                vt[i] = R("v", lambda e, s_=s_, kc=kc, bank=bank: e.reduce_max(
                    out=n.cm[:, 8 * s_ + kc:8 * s_ + kc + 1], in_=bank[:], axis=AX.X), wait=[mmt[i]])
        ch.join([vt[len(jobs) - 1], mmt[len(jobs) - 1]])
        for s_ in range(4):
            qa0 = q0 + 128 * s_
            nk = (qa0 + 128 + 511) // 512
            if cfg["decay"]:
                ch.run("v", lambda e, s_=s_, nk=nk: e.tensor_tensor(
                    out=n.cm[:, 8 * s_:8 * s_ + nk], in0=n.cm[:, 8 * s_:8 * s_ + nk],
                    in1=n.kcoff[:, 0:nk], op=ALU.add), small=True, fence=True)
            ch.run("v", lambda e, s_=s_, nk=nk: e.reduce_max(
                out=n.Mv[:], in_=n.cm[:, 8 * s_:8 * s_ + nk], axis=AX.X), small=True)
            put_neg_m_row(ch, n, s_)

    nkc = (q0 + 512) // 128 if causal else 32
    vcd = {}
    bank_free = {}
    stmp_free = {}
    pt_ready = {}
    pt_free = {}
    pvt = {}
    cur_of = {}
    start_tok = ch.last

    def emit_st_chain(kc):
        bank = pb[kc % 2]
        w = []
        if kc >= 2:
            w.append(bank_free[kc - 2])
            if cfg["topk"] or (cfg["decay"] and causal):
                if (kc - 2) in stmp_free:
                    w.append(stmp_free[kc - 2])
        elif start_tok is not None:
            w.append(start_tok)
        if kc >= 2 and (kc - 2) in pt_free:
            w.append(pt_free[kc - 2])
        if cfg["topk"]:
            R("t", lambda e, kc=kc, bank=bank: e.matmul(
                out=bank[:], lhsT=n.kh[0:65, 128 * kc:128 * (kc + 1)], rhs=n.qa[0:65, :],
                start=True, stop=False), wait=w)
            R("t", lambda e, kc=kc, bank=bank: e.matmul(
                out=bank[:], lhsT=n.khlo[0:65, 128 * kc:128 * (kc + 1)], rhs=n.qa[0:65, :],
                start=False, stop=False), wait=[])
            tmm = R("t", lambda e, kc=kc, bank=bank: e.matmul(
                out=bank[:], lhsT=n.kh[0:65, 128 * kc:128 * (kc + 1)], rhs=n.qalo[0:65, :],
                start=False, stop=True), wait=[])
        else:
            tmm = R("t", lambda e, kc=kc, bank=bank: e.matmul(
                out=bank[:], lhsT=n.kh[0:65, 128 * kc:128 * (kc + 1)], rhs=n.qa[0:65, :],
                start=True, stop=True), wait=w)
        p = kc % 2
        PT, PT2, stmp = n.PTr[p], n.PT2r[p], n.stmpr[p]
        bias = n.poskk[:, kc:kc + 1] if cfg["decay"] else 0.0
        dec_diag = cfg["decay"] and causal and 128 * kc >= q0
        if cfg["topk"]:
            dkq = 128 * kc - q0
            if -638 <= dkq <= 1022:
                v0k = 1149 + dkq
                skk = n.skk[:]
                srck = RawAP(skk.tensor, skk.offset + v0k, [[skk.ap[0][0], 128], [-1, 512]])
                tb = R("v", lambda e, bank=bank, srck=srck, stmp=stmp: e.scalar_tensor_tensor(
                    out=stmp[:], in0=bank[:], scalar=1.0, in1=srck, op0=ALU.mult, op1=ALU.add),
                    wait=[tmm])
            else:
                fcol = 1 if dkq > 1022 else 0
                tb = R("v", lambda e, bank=bank, fcol=fcol, stmp=stmp: e.tensor_scalar(
                    out=stmp[:], in0=bank[:], scalar1=n.farv[:, fcol:fcol + 1],
                    scalar2=None, op0=ALU.add), wait=[tmm])
            bank_free[kc] = tb
            te = R("s", lambda e, stmp=stmp, PT=PT: e.activation(
                out=PT[:], in_=stmp[:], func=AF.Exp, bias=0.0, scale=1.0), wait=[tb])
            tg = R("v", lambda e, stmp=stmp, p=p: e.tensor_tensor(
                out=n.tjunk[:, 512 * p:512 * (p + 1)], in0=stmp[:], in1=n.taub[:], op=ALU.is_ge), wait=[])
            tm = R("v", lambda e, PT=PT, PT2=PT2, p=p: e.tensor_tensor(
                out=PT2[:], in0=PT[:], in1=n.tjunk[:, 512 * p:512 * (p + 1)], op=ALU.mult), wait=[te])
            stmp_free[kc] = tg
            pt_ready[kc] = tm
            cur_of[kc] = PT2
        elif dec_diag:
            d_ = kc - q0 // 128
            tb = R("v", lambda e, bank=bank, d_=d_, stmp=stmp: e.scalar_tensor_tensor(
                out=stmp[:], in0=bank[:], scalar=1.0, in1=n.trikq[d_][:],
                op0=ALU.mult, op1=ALU.add), wait=[tmm])
            bank_free[kc] = tb
            stmp_free[kc] = tb
            te = R("s", lambda e, bias=bias, stmp=stmp, PT=PT: e.activation(
                out=PT[:], in_=stmp[:], func=AF.Exp, bias=bias, scale=1.0), wait=[tb])
            stmp_free[kc] = te
            pt_ready[kc] = te
            cur_of[kc] = PT
        else:
            te = R("s", lambda e, bank=bank, bias=bias, PT=PT: e.activation(
                out=PT[:], in_=bank[:], func=AF.Exp, bias=bias, scale=1.0), wait=[tmm])
            bank_free[kc] = te
            if causal and 128 * kc >= q0:
                d_ = kc - q0 // 128
                tm = R("v", lambda e, PT=PT, PT2=PT2, d_=d_: e.tensor_tensor(
                    out=PT2[:], in0=PT[:], in1=n.trik[d_][:], op=ALU.mult), wait=[te])
                pt_ready[kc] = tm
                cur_of[kc] = PT2
            else:
                pt_ready[kc] = te
                cur_of[kc] = PT

    def emit_pv(kc):
        cur = cur_of[kc]
        p = kc % 2
        w = [pt_ready[kc], vcd[kc]]
        first = True
        for s_ in range(4):
            ob = pb[2 + s_]
            R("t", lambda e, s_=s_, kc=kc, cur=cur, ob=ob, h=h, p=p: e.matmul(
                out=ob[:, 0:64], lhsT=cur[:, 128 * s_:128 * (s_ + 1)],
                rhs=n.vcr[p][:, 64 * h:64 * (h + 1)], start=(kc == 0), stop=False),
              wait=(w if first else []))
            first = False
            tok = R("t", lambda e, s_=s_, kc=kc, cur=cur, ob=ob, nkc=nkc, p=p: e.matmul(
                out=ob[:, 64:65], lhsT=cur[:, 128 * s_:128 * (s_ + 1)],
                rhs=n.onescol_b[:], start=False, stop=(kc == nkc - 1)), wait=[])
        pvt[kc] = tok
        pt_free[kc] = tok

    for kc in range(nkc):
        wv = [pvt[kc - 2]] if kc >= 2 else ([start_tok] if start_tok else [])
        vcd[kc] = R("sync", lambda e, kc=kc, p=kc % 2: e.dma_start(
            out=n.vcr[p][:], in_=n.dr.vt_d[128 * kc:128 * (kc + 1), :]), wait=wv, dma=True)
        emit_st_chain(kc)
        if kc >= 1:
            emit_pv(kc - 1)
    emit_pv(nkc - 1)
    ch.join([pvt[nkc - 1], bank_free[nkc - 1], bank_free.get(nkc - 2),
             stmp_free.get(nkc - 1), stmp_free.get(nkc - 2)])

    for s_ in range(4):
        ob = pb[2 + s_]
        qq = 4 * qb + s_
        ch.run("v", lambda e, ob=ob, s_=s_: e.reciprocal(out=n.rz[:, s_:s_ + 1], in_=ob[:, 64:65]), small=True)
        if cfg["gates"]:
            ch.run("v", lambda e, s_=s_, h=h: e.tensor_tensor(
                out=n.rz[:, s_:s_ + 1], in0=n.rz[:, s_:s_ + 1], in1=n.gate_bc[:, h:h + 1],
                op=ALU.mult), small=True, fence=True)
        ch.run("v", lambda e, ob=ob, s_=s_, qq=qq, h=h: e.tensor_scalar(
            out=n.osb[:, 512 * qq + 64 * h:512 * qq + 64 * (h + 1)], in0=ob[:, 0:64],
            scalar1=n.rz[:, s_:s_ + 1], scalar2=None, op0=ALU.mult), small=True)


def emit_l1a_stats(ch, cfg, n, h, qb, s_):
    pb = n.pb
    q0 = 512 * qb
    qa0 = q0 + 128 * s_
    for kc in range(8):
        bank = pb[kc % 2]
        ch.run("t", lambda e, kc=kc, s_=s_, bank=bank: e.matmul(
            out=bank[:], lhsT=n.qa[0:64, 128 * s_:128 * (s_ + 1)],
            rhs=n.kh[0:64, 512 * kc:512 * kc + 512], start=True, stop=False))
        ch.run("t", lambda e, kc=kc, s_=s_, bank=bank: e.matmul(
            out=bank[:], lhsT=n.qalo[0:64, 128 * s_:128 * (s_ + 1)],
            rhs=n.kh[0:64, 512 * kc:512 * kc + 512], start=False, stop=False))
        ch.run("t", lambda e, kc=kc, s_=s_, bank=bank: e.matmul(
            out=bank[:], lhsT=n.qa[0:64, 128 * s_:128 * (s_ + 1)],
            rhs=n.khlo[0:64, 512 * kc:512 * kc + 512], start=False, stop=True))
        dk = 512 * kc - qa0
        if -1152 <= dk <= 640:
            v0 = TMAX - dk
            skq = n.skq[:]
            src = RawAP(skq.tensor, skq.offset + v0, [[skq.ap[0][0], 128], [-1, 512]])
            ch.run("v", lambda e, bank=bank, src=src, kc=kc: e.scalar_tensor_tensor(
                out=n.ssb[:, 512 * kc:512 * (kc + 1)], in0=bank[:], scalar=1.0,
                in1=src, op0=ALU.mult, op1=ALU.add))
        else:
            fcol = 1 if dk > 640 else 0
            ch.run("v", lambda e, bank=bank, kc=kc, fcol=fcol: e.tensor_scalar(
                out=n.ssb[:, 512 * kc:512 * (kc + 1)], in0=bank[:],
                scalar1=n.farv[:, fcol:fcol + 1], scalar2=None, op0=ALU.add))
    ch.run("v", lambda e: e.tensor_reduce(out=n.segm[:], in_=n.ssb[:].rearrange("p (a b) -> p a b", a=64),
                                          axis=AX.X, op=ALU.max), small=True)
    ch.run("v", lambda e: e.reduce_max(out=n.Mv[:], in_=n.segm[:], axis=AX.X), small=True)
    ch.run("v", lambda e: e.tensor_reduce(out=n.lov[:], in_=n.segm[:], axis=AX.X, op=ALU.min), small=True)
    ch.run("v", lambda e: e.tensor_copy(out=n.hiv[:], in_=n.Mv[:]), small=True)
    for it in range(NITER_TOPK):
        ch.run("v", lambda e: e.tensor_scalar(out=n.sm[:, 9:10], in0=n.lov[:], scalar1=n.hiv[:],
                                              scalar2=-0.5, op0=ALU.add, op1=ALU.mult), small=True)
        ch.run("v", lambda e: e.tensor_scalar(out=n.midv[:], in0=n.sm[:, 9:10], scalar1=-1.0,
                                              scalar2=None, op0=ALU.mult), small=True)
        ch.run("v", lambda e: e.tensor_scalar(out=n.tjunk[:], in0=n.ssb[:, 0:S // 2], scalar1=n.midv[:],
                                              scalar2=None, op0=ALU.is_ge, op1=ALU.add,
                                              accum_out=n.cntv[:]), small=True)
        ch.run("v", lambda e: e.tensor_scalar(out=n.tjunk[:], in0=n.ssb[:, S // 2:], scalar1=n.midv[:],
                                              scalar2=None, op0=ALU.is_ge, op1=ALU.add,
                                              accum_out=n.sm[:, 11:12]), small=True, fence=True)
        ch.run("v", lambda e: e.tensor_tensor(out=n.cntv[:], in0=n.cntv[:], in1=n.sm[:, 11:12],
                                              op=ALU.add), small=True, fence=True)
        ch.run("v", lambda e: e.tensor_scalar(out=n.mskv[:], in0=n.cntv[:], scalar1=64.0,
                                              scalar2=None, op0=ALU.is_ge), small=True)
        ch.run("v", lambda e: e.copy_predicated(out=n.lov[:], mask=n.mskv[:], data=n.midv[:]), small=True, fence=True)
        ch.run("v", lambda e: e.tensor_scalar(out=n.mskv[:], in0=n.cntv[:], scalar1=64.0,
                                              scalar2=None, op0=ALU.is_lt), small=True)
        ch.run("v", lambda e: e.copy_predicated(out=n.hiv[:], mask=n.mskv[:], data=n.midv[:]), small=True, fence=True)

    ch.run("v", lambda e: e.tensor_scalar(out=n.mrow_b[:], in0=n.Mv[:], scalar1=-1.0,
                                          scalar2=None, op0=ALU.mult), small=True)
    ch.run("v", lambda e: e.tensor_copy(out=n.Mb32[:], in_=n.mrow_b[:]), small=True)
    ch.run("v", lambda e: e.tensor_tensor(out=n.sm[:, 10:11], in0=n.lov[:], in1=n.Mb32[:], op=ALU.add), small=True)
    ch.run("t", lambda e: e.transpose(out=n.pbc[0:1, 0:128], in_=n.sm[:, 10:11], identity=n.identf[:]))
    ch.run("v", lambda e, s_=s_: e.tensor_copy(out=n.taur_f[0:1, 128 * s_:128 * (s_ + 1)],
                                               in_=n.pbc[0:1, 0:128]), small=True)
    ch.run("t", lambda e: e.transpose(out=n.pt_bf[0:1, 0:128], in_=n.mrow_b[:], identity=n.ident[:]))
    ch.run("v", lambda e, s_=s_: e.tensor_copy(out=n.qa[64:65, 128 * s_:128 * (s_ + 1)],
                                               in_=n.pt_bf[0:1, 0:128]), small=True)
    if s_ == 3:
        ch.run("t", lambda e: e.matmul(out=n.pbc[:], lhsT=n.onescol_f[0:1, :],
                                       rhs=n.taur_f[0:1, :], start=True, stop=True))
        ch.run("v", lambda e: e.tensor_copy(out=n.taub[:], in_=n.pbc[:]))



def build_ffn():
    nc = bass.Bass("TRN2", target_bir_lowering=False)
    ap = {}
    ap["x"] = nc.dram_tensor("x", [ROWS, D], F32, kind="ExternalInput").ap()
    ap["w"] = nc.dram_tensor("w", [D, D], F32, kind="ExternalInput").ap()
    ap["ident"] = nc.dram_tensor("ident", [128, 128], BF16, kind="ExternalInput").ap()
    xo = nc.dram_tensor("xo", [ROWS, D], F32, kind="ExternalOutput").ap()
    st = ExitStack()
    with st:
        ch = Ch(nc)
        ch.open_sems(st)
        ffn_stage(nc, ch, st, ap, xo, "")
        with nc.Block() as block:
            ch.flush(block)
    return nc


def ffn_stage(nc, ch, stage_st, ap, xo, pfx):
    st = stage_st
    if True:
        n = NS()
        def sb(name, shape, dt):
            t = st.enter_context(nc.sbuf_tensor(pfx + "sb_" + name, shape, dt))
            n[name] = t
            return t
        sb("ident", [128, 128], BF16)
        n.wsT = [st.enter_context(nc.sbuf_tensor(pfx + f"wsT{k}", [128, D], BF16)) for k in range(8)]
        sb("wtmp_f", [128, D], F32); sb("wtmp_b", [128, D], BF16)
        sb("x_t", [128, D], F32); sb("dump", [128, D], F32)
        sb("sm", [128, 16], F32); sb("epsv", [128, 1], F32)
        sb("xn_t", [128, D], BF16); sb("xn_f", [128, D], BF16)
        n.pb = [st.enter_context(nc.psum_tensor(pfx + f"pb{i}", [128, 512], F32)) for i in range(2)]
        n.pt_bf = st.enter_context(nc.psum_tensor(pfx + "pt_bf", [128, 1024], BF16))
        n.ap = ap
        ch.run("v", lambda e: e.memset(n.epsv[:], float(EPS)), small=True)
        ch.run("sync", lambda e: e.dma_start(out=n.ident[:], in_=ap["ident"]), dma=True)
        load_wT(ch, n, ap["w"], n.wsT, D, sign=True)
        for t in range(ROWS // 128):
            ch.run("sync", lambda e, t=t: e.dma_start(out=n.x_t[:], in_=ap["x"][128 * t:128 * (t + 1), :]), dma=True)
            ln_tile(ch, n, n.x_t[:], n.xn_t[:])
            transpose_128(ch, n, lambda k: n.xn_f[:, 128 * k:128 * (k + 1)],
                          lambda k: n.xn_t[:, 128 * k:128 * (k + 1)], 8)
            for og in range(2):
                bank = n.pb[og]
                for k in range(8):
                    ch.run("t", lambda e, og=og, k=k, bank=bank: e.matmul(
                        out=bank[:], lhsT=n.xn_f[:, 128 * k:128 * (k + 1)],
                        rhs=n.wsT[k][:, 512 * og:512 * (og + 1)], start=(k == 0), stop=(k == 7)))
                ch.run("v", lambda e, og=og, bank=bank: e.scalar_tensor_tensor(
                    out=n.x_t[:, 512 * og:512 * (og + 1)], in0=bank[:], scalar=1.0,
                    in1=n.x_t[:, 512 * og:512 * (og + 1)], op0=ALU.mult, op1=ALU.add), fence=True)
            ch.run("sync", lambda e, t=t: e.dma_start(out=xo[128 * t:128 * (t + 1), :], in_=n.x_t[:]), dma=True)
        ch.run("gp", lambda e: e.memset(n.sm[:, 8:9], 0.0))



def build_moe():
    nc = bass.Bass("TRN2", target_bir_lowering=False)
    ap = {}
    def din(name, shape, dt=F32):
        ap[name] = nc.dram_tensor(name, shape, dt, kind="ExternalInput").ap()
    din("x", [ROWS, D])
    din("wgs", [9, 2048, D])
    din("wus", [9, 2048, D])
    din("wds", [9, D, 2048])
    din("rw", [8, D])
    din("rb", [1, 8])
    din("ident", [128, 128], BF16)
    xo = nc.dram_tensor("xo", [ROWS, D], F32, kind="ExternalOutput").ap()
    st = ExitStack()
    with st:
        ch = Ch(nc)
        ch.open_sems(st)
        moe_stage(nc, ch, st, ap, xo, "")
        with nc.Block() as block:
            ch.flush(block)
    return nc


def moe_stage(nc, ch, st, ap, xo, pfx):
    xn_d = nc.dram_tensor(pfx + "xn_d", [ROWS, D], BF16).ap()
    if True:
        n = NS()
        def sb(name, shape, dt):
            t = st.enter_context(nc.sbuf_tensor(pfx + "sb_" + name, shape, dt))
            n[name] = t
            return t
        n.xo_ap = xo
        n.xn_d_ap = xn_d
        sb("ident", [128, 128], BF16)
        sb("wtmp_f", [128, D], F32); sb("wtmp_b", [128, 2048], BF16)
        n.wtf = [st.enter_context(nc.sbuf_tensor(pfx + f"wtf{i}", [128, 2048], F32)) for i in range(2)]
        n.wtb = [st.enter_context(nc.sbuf_tensor(pfx + f"wtb{i}", [128, 2048], BF16)) for i in range(2)]
        sb("x_t", [128, D], F32); sb("dump", [128, D], F32)
        sb("sm", [128, 16], F32); sb("epsv", [128, 1], F32)
        sb("xn_t", [128, D], BF16)
        n.xnf = [st.enter_context(nc.sbuf_tensor(pfx + f"xnf{k}", [128, 1024], BF16)) for k in range(8)]
        n.A1 = [st.enter_context(nc.sbuf_tensor(pfx + f"A1_{k}", [128, 2048], BF16)) for k in range(8)]
        n.A2 = [st.enter_context(nc.sbuf_tensor(pfx + f"A2_{k}", [128, D], BF16)) for k in range(16)]
        n.gs = [st.enter_context(nc.sbuf_tensor(pfx + f"gs_{k}", [128, 1024], BF16)) for k in range(16)]
        n.ya = [st.enter_context(nc.sbuf_tensor(pfx + f"ya_{k}", [128, D], F32)) for k in range(8)]
        n.rwT = [st.enter_context(nc.sbuf_tensor(pfx + f"rwT{k}", [128, 8], BF16)) for k in range(8)]
        n.rwTlo = [st.enter_context(nc.sbuf_tensor(pfx + f"rwTlo{k}", [128, 8], BF16)) for k in range(8)]
        sb("xn32", [128, D], F32); sb("xnlo_t", [128, D], BF16)
        sb("rb_bc", [128, 8], F32); sb("rb_r", [1, 8], F32)
        sb("onescol_f", [1, 128], F32)
        sb("probs", [128, 8], F32); sb("p2", [128, 8], F32)
        sb("w8", [128, 8], F32)
        sb("mx", [128, 4], F32)
        sb("wc", [128, 9 * 16], F32)
        n.pb = [st.enter_context(nc.psum_tensor(pfx + f"pb{i}", [128, 512], F32)) for i in range(4)]
        n.pbc = st.enter_context(nc.psum_tensor(pfx + "pbc", [128, 512], F32))
        n.pt_bf = st.enter_context(nc.psum_tensor(pfx + "pt_bf", [128, 2048], BF16))
        n.ap = ap
        emit_moe(ch, n)


def emit_moe(ch, n):
    ap = n.ap
    xo = None
    ch.run("v", lambda e: e.memset(n.epsv[:], float(EPS)), small=True)
    ch.run("v", lambda e: e.memset(n.onescol_f[:], 1.0), small=True)
    ch.run("sync", lambda e: e.dma_start(out=n.ident[:], in_=ap["ident"]), dma=True)
    ch.run("sync", lambda e: e.dma_start(out=n.rb_r[:], in_=ap["rb"]), dma=True)
    bcast_row(ch, n, n.rb_bc, n.rb_r[:], 8)
    ch.run("sync", lambda e: e.dma_start(out=n.wtmp_f[0:8, 0:D], in_=ap["rw"]), dma=True)
    ch.run("v", lambda e: e.tensor_copy(out=n.wtmp_b[0:8, 0:D], in_=n.wtmp_f[0:8, 0:D]))
    ch.run("v", lambda e: e.tensor_tensor(out=n.dump[0:8, 0:D], in0=n.wtmp_f[0:8, 0:D],
                                          in1=n.wtmp_b[0:8, 0:D], op=ALU.subtract))
    ch.run("v", lambda e: e.tensor_copy(out=n.wtmp_b[0:8, D:2 * D], in_=n.dump[0:8, 0:D]), fence=True)
    for k in range(8):
        ch.run("t", lambda e, k=k: e.transpose(out=n.pt_bf[:, 0:8], in_=n.wtmp_b[0:8, 128 * k:128 * (k + 1)],
                                               identity=n.ident[0:8, 0:8]))
        ch.run("v", lambda e, k=k: e.tensor_copy(out=n.rwT[k][:], in_=n.pt_bf[:, 0:8]), small=True)
        ch.run("t", lambda e, k=k: e.transpose(out=n.pt_bf[:, 0:8], in_=n.wtmp_b[0:8, D + 128 * k:D + 128 * (k + 1)],
                                               identity=n.ident[0:8, 0:8]))
        ch.run("v", lambda e, k=k: e.tensor_copy(out=n.rwTlo[k][:], in_=n.pt_bf[:, 0:8]), small=True)
    for t in range(ROWS // 128):
        ch.run("sync", lambda e, t=t: e.dma_start(out=n.x_t[:], in_=ap["x"][128 * t:128 * (t + 1), :]), dma=True)
        ln_tile(ch, n, n.x_t[:], n.xn32[:])
        ch.run("v", lambda e: e.tensor_copy(out=n.xn_t[:], in_=n.xn32[:]))
        ch.run("v", lambda e: e.tensor_tensor(out=n.dump[:], in0=n.xn32[:], in1=n.xn_t[:], op=ALU.subtract))
        ch.run("v", lambda e: e.tensor_copy(out=n.xnlo_t[:], in_=n.dump[:]), fence=True)
        ch.run("sync", lambda e, t=t: e.dma_start(out=n.xn_d_ap[128 * t:128 * (t + 1), :], in_=n.xn_t[:]), dma=True)
        transpose_128(ch, n, lambda k: n.xnf[k][:, 0:128], lambda k: n.xn_t[:, 128 * k:128 * (k + 1)], 8)
        transpose_128(ch, n, lambda k: n.wtmp_b[:, 128 * k:128 * (k + 1)],
                      lambda k: n.xnlo_t[:, 128 * k:128 * (k + 1)], 8)
        bank = n.pbc
        for k in range(8):
            ch.run("t", lambda e, k=k, bank=bank: e.matmul(
                out=bank[:, 0:8], lhsT=n.xnf[k][:, 0:128], rhs=n.rwT[k][:], start=(k == 0), stop=False))
        for k in range(8):
            ch.run("t", lambda e, k=k, bank=bank: e.matmul(
                out=bank[:, 0:8], lhsT=n.wtmp_b[:, 128 * k:128 * (k + 1)], rhs=n.rwT[k][:], start=False, stop=False))
        for k in range(8):
            ch.run("t", lambda e, k=k, bank=bank: e.matmul(
                out=bank[:, 0:8], lhsT=n.xnf[k][:, 0:128], rhs=n.rwTlo[k][:], start=False, stop=(k == 7)))
        ch.run("v", lambda e, bank=bank: e.scalar_tensor_tensor(
            out=n.probs[:], in0=bank[:, 0:8], scalar=1.0, in1=n.rb_bc[:], op0=ALU.mult, op1=ALU.add), small=True)
        ch.run("s", lambda e: e.activation(out=n.probs[:], in_=n.probs[:], func=AF.Sigmoid,
                                           bias=0.0, scale=1.0), small=True, fence=True)
        ch.run("v", lambda e: e.reduce_max(out=n.mx[:, 0:1], in_=n.probs[:], axis=AX.X), small=True)
        ch.run("v", lambda e: e.tensor_scalar(out=n.p2[:], in0=n.probs[:], scalar1=n.mx[:, 0:1],
                                              scalar2=None, op0=ALU.is_ge), small=True)
        ch.run("v", lambda e: e.scalar_tensor_tensor(out=n.p2[:], in0=n.p2[:], scalar=-1e30,
                                                     in1=n.probs[:], op0=ALU.mult, op1=ALU.add), small=True, fence=True)
        ch.run("v", lambda e: e.reduce_max(out=n.mx[:, 1:2], in_=n.p2[:], axis=AX.X), small=True)
        ch.run("v", lambda e: e.tensor_tensor(out=n.mx[:, 2:3], in0=n.mx[:, 0:1], in1=n.mx[:, 1:2], op=ALU.add), small=True)
        ch.run("v", lambda e: e.reciprocal(out=n.mx[:, 3:4], in_=n.mx[:, 2:3]), small=True)
        ch.run("v", lambda e: e.tensor_scalar(out=n.w8[:], in0=n.probs[:], scalar1=n.mx[:, 1:2],
                                              scalar2=None, op0=ALU.is_ge), small=True)
        ch.run("v", lambda e: e.tensor_tensor(out=n.w8[:], in0=n.w8[:], in1=n.probs[:], op=ALU.mult), small=True, fence=True)
        ch.run("v", lambda e: e.tensor_scalar(out=n.w8[:], in0=n.w8[:], scalar1=n.mx[:, 3:4],
                                              scalar2=None, op0=ALU.mult), small=True, fence=True)
        ch.run("v", lambda e, t=t: e.tensor_copy(out=n.wc[:, 9 * t:9 * t + 8], in_=n.w8[:]), small=True)
        ch.run("v", lambda e, t=t: e.memset(n.wc[:, 9 * t + 8:9 * t + 9], 1.0), small=True)
    R = lambda e, f, **kw: ch.run(e, f, chain=False, **kw)

    def load_mat(which, ex, dst, ntiles, nch, w_first, pe_after):
        tcast = {}
        tcopy = {}
        tpe = pe_after
        for j in range(ntiles):
            wf = n.wtf[j % 2]
            wb = n.wtb[j % 2]
            w = [tcast.get(j - 2)] if j >= 2 else list(w_first)
            if which == "wds":
                R("sync", lambda e, ex=ex, j=j, wf=wf: e.dma_start(
                    out=wf[:, 0:1024], in_=ap["wds"][ex, 128 * j:128 * (j + 1), 0:1024]), wait=w, dma=True)
                td = R("sync", lambda e, ex=ex, j=j, wf=wf: e.dma_start(
                    out=wf[:, 1024:2048], in_=ap["wds"][ex, 128 * j:128 * (j + 1), 1024:2048]), wait=[], dma=True)
            else:
                td = R("sync", lambda e, ex=ex, j=j, wf=wf: e.dma_start(
                    out=wf[:, 0:1024], in_=ap[which][ex, 128 * j:128 * (j + 1), :]), wait=w, dma=True)
            cw = [td] + ([tcopy[j - 2]] if j >= 2 else [])
            tcast[j] = R("v", lambda e, wf=wf, wb=wb, nch=nch: e.tensor_copy(
                out=wb[:, 0:128 * nch], in_=wf[:, 0:128 * nch]), wait=cw)
            so = (j % 2) * 8 if nch == 8 else 0
            tw = [tcast[j]]
            if nch == 16 and j >= 1:
                tw.append(tcopy[j - 1])
            elif nch == 8 and j >= 2:
                tw.append(tcopy[j - 2])
            if tpe is not None:
                tw.append(tpe)
                tpe = None
            first = True
            for k in range(nch):
                t_tr = R("t", lambda e, k=k, so=so, wb=wb: e.transpose(
                    out=n.pt_bf[:, 128 * ((so + k) % 16):128 * ((so + k) % 16 + 1)],
                    in_=wb[:, 128 * k:128 * (k + 1)], identity=n.ident[:]), wait=(tw if first else []))
                first = False
            last = None
            for k in range(nch):
                last = R("v", lambda e, k=k, so=so, j=j, dst=dst: e.tensor_copy(
                    out=dst[k][:, 128 * j:128 * (j + 1)],
                    in_=n.pt_bf[:, 128 * ((so + k) % 16):128 * ((so + k) % 16 + 1)]), wait=([t_tr] if k == 0 else []))
            tcopy[j] = last
        return tcopy[ntiles - 1], t_tr

    for half in range(2):
        tb = 8 * half
        for t in range(8):
            ch.run("sync", lambda e, t=t, tb=tb: e.dma_start(out=n.xn_t[:], in_=n.xn_d_ap[128 * (tb + t):128 * (tb + t + 1), :]), dma=True)
            transpose_128(ch, n, lambda k, t=t: n.xnf[k][:, 128 * t:128 * (t + 1)],
                          lambda k: n.xn_t[:, 128 * k:128 * (k + 1)], 8)
            ch.run("sync", lambda e, t=t, tb=tb: e.dma_start(out=n.ya[t][:], in_=ap["x"][128 * (tb + t):128 * (tb + t + 1), :]), dma=True)
        base = ch.last
        tA1_free = base
        tA2_free = base
        t_y_done = base
        t_y_done2 = base
        t_sil = {}
        for ex in range(9):
            lw, lt = load_mat("wgs", ex, n.A1, 16, 8, [tA1_free], None)
            bank_tok = {}
            for i in range(32):
                ic, tb2 = i // 2, i % 2
                bank = n.pb[i % 2]
                w = [bank_tok[i - 2]] if i >= 2 else [lw, t_y_done, t_y_done2]
                for k in range(8):
                    tm = R("t", lambda e, ic=ic, tb2=tb2, k=k, bank=bank: e.matmul(
                        out=bank[:], lhsT=n.A1[k][:, 128 * ic:128 * (ic + 1)],
                        rhs=n.xnf[k][:, 512 * tb2:512 * (tb2 + 1)], start=(k == 0), stop=(k == 7)),
                        wait=(w if k == 0 else []))
                bank_tok[i] = R("s", lambda e, ic=ic, tb2=tb2, bank=bank: e.activation(
                    out=n.gs[ic][:, 512 * tb2:512 * (tb2 + 1)], in_=bank[:], func=AF.Silu,
                    bias=0.0, scale=1.0), wait=[tm])
                t_sil[i] = bank_tok[i]
            tg_mm_last = tm
            lw, lt = load_mat("wus", ex, n.A1, 16, 8, [tg_mm_last], None)
            for i in range(32):
                ic, tb2 = i // 2, i % 2
                bank = n.pb[i % 2]
                w = [bank_tok[i - 2]] if i >= 2 else [lw, bank_tok[30 + i]]
                for k in range(8):
                    tm = R("t", lambda e, ic=ic, tb2=tb2, k=k, bank=bank: e.matmul(
                        out=bank[:], lhsT=n.A1[k][:, 128 * ic:128 * (ic + 1)],
                        rhs=n.xnf[k][:, 512 * tb2:512 * (tb2 + 1)], start=(k == 0), stop=(k == 7)),
                        wait=(w if k == 0 else []))
                bank_tok[i] = R("v", lambda e, ic=ic, tb2=tb2, bank=bank: e.tensor_tensor(
                    out=n.gs[ic][:, 512 * tb2:512 * (tb2 + 1)], in0=bank[:],
                    in1=n.gs[ic][:, 512 * tb2:512 * (tb2 + 1)], op=ALU.mult),
                    wait=[tm, t_sil[i]])
            tu_mm_last = tm
            th_last = bank_tok[31]
            lwd, ltd = load_mat("wds", ex, n.A2, 8, 16, [tA2_free], None)
            tA1_free = tu_mm_last
            ytok = {}
            ylast = None
            for i in range(16):
                t_, og = i // 2, i % 2
                bank = n.pb[i % 2]
                w = [ytok[i - 2]] if i >= 2 else [lwd, th_last, bank_tok[30 + i]]
                for ic in range(16):
                    tm = R("t", lambda e, t_=t_, og=og, ic=ic, bank=bank: e.matmul(
                        out=bank[:], lhsT=n.gs[ic][:, 128 * t_:128 * (t_ + 1)],
                        rhs=n.A2[ic][:, 512 * og:512 * (og + 1)], start=(ic == 0), stop=(ic == 15)),
                        wait=(w if ic == 0 else []))
                prev_acc = ytok.get(i - 2 if False else i - 16)
                yw = [tm]
                if i >= 2:
                    pass
                if ylast is not None and i >= 14:
                    pass
                ytok[i] = R("v", lambda e, t_=t_, og=og, ex=ex, tb=tb, bank=bank: e.scalar_tensor_tensor(
                    out=n.ya[t_][:, 512 * og:512 * (og + 1)], in0=bank[:],
                    scalar=n.wc[:, 9 * (tb + t_) + ex:9 * (tb + t_) + ex + 1],
                    in1=n.ya[t_][:, 512 * og:512 * (og + 1)], op0=ALU.mult, op1=ALU.add), wait=yw)
                ylast = ytok[i]
            tA2_free = tm
            t_y_done = ylast
            t_y_done2 = ytok[14]
        ch.join([ylast, tm, ch.tok("v"), ch.tok("s"), ch.tok("sync")])
        for t in range(8):
            ch.run("sync", lambda e, t=t, tb=tb: e.dma_start(out=n.xo_ap[128 * (tb + t):128 * (tb + t + 1), :], in_=n.ya[t][:]), dma=True)
    ch.run("gp", lambda e: e.memset(n.sm[:, 8:9], 0.0))



def build_fused(cfgs, dbg_mid=False):
    nc = bass.Bass("TRN2", target_bir_lowering=False)
    g = {}
    def din(name, shape, dt=F32):
        if name not in g:
            g[name] = nc.dram_tensor(name, shape, dt, kind="ExternalInput").ap()
        return g[name]
    x_in = nc.dram_tensor("x", [ROWS, D], F32, kind="ExternalInput").ap()
    xo = nc.dram_tensor("xo", [ROWS, D], F32, kind="ExternalOutput").ap()
    xd = [nc.dram_tensor(f"xmid{i}", [ROWS, D], F32, kind="ExternalOutput" if dbg_mid else "Internal").ap() for i in range(7)]
    din("ident", [128, 128], BF16)
    din("identf", [128, 128])
    for s_ in range(4):
        din(f"triq_{s_}", [128, 512])
        din(f"trik_{s_}", [128, 512], BF16)
        din(f"trikq_{s_}", [128, 512])
    din("poskq", [128, 512]); din("poskk", [128, 32])
    din("ropec", [S, DH // 2]); din("ropes", [S, DH // 2])

    stage_list = [("a0_", "attn", "l0a"), ("f0_", "ffn", None), ("b0_", "attn", "l0b"),
                  ("f1_", "ffn", None), ("c0_", "attn", "l1a"), ("m0_", "moe", None),
                  ("d0_", "attn", "l1b"), ("m1_", "moe", None)]
    with ExitStack() as top:
        ch = Ch(nc)
        ch.open_sems(top)
        for i, (pfx, kind, ckey) in enumerate(stage_list):
            x_ap = x_in if i == 0 else xd[i - 1]
            y_ap = xo if i == 7 else xd[i]
            with ExitStack() as st:
                if kind == "attn":
                    cfg = dict(cfgs[ckey])
                    if dbg_mid and ckey == "l1a":
                        cfg["dbg"] = True
                    ap = {"x": x_ap, "ident": g["ident"]}
                    ap["wqkv"] = din(pfx + "wqkv", [3 * MYF, D])
                    ap["bqkv"] = din(pfx + "bqkv", [1, 3 * MYF])
                    ap["wo"] = din(pfx + "wo", [D, MYF])
                    ap["bo"] = din(pfx + "bo", [1, D])
                    if cfg["causal"]:
                        for s_ in range(4):
                            ap[f"triq_{s_}"] = g[f"triq_{s_}"]
                            ap[f"trik_{s_}"] = g[f"trik_{s_}"]
                    if cfg["decay"]:
                        ap["poskq"] = g["poskq"]; ap["poskk"] = g["poskk"]
                        for s_ in range(4):
                            ap[f"trikq_{s_}"] = g[f"trikq_{s_}"]
                    if cfg["rope"]:
                        ap["ropec"] = g["ropec"]; ap["ropes"] = g["ropes"]
                    if cfg["topk"]:
                        ap["identf"] = g["identf"]
                    if cfg["relbias"]:
                        ap["bexr"] = din(pfx + "bexr", [MYH, LQT])
                        ap["bex"] = din(pfx + "bex", [MYH, LQT])
                        ap["bfar"] = din(pfx + "bfar", [1, 2 * MYH])
                    if cfg["gates"]:
                        ap["gates"] = din(pfx + "gates", [1, MYH])
                    attn_stage(nc, ch, st, cfg, ap, y_ap, pfx)
                elif kind == "ffn":
                    ap = {"x": x_ap, "ident": g["ident"],
                          "w": din("ffn_w", [D, D])}
                    ffn_stage(nc, ch, st, ap, y_ap, pfx)
                else:
                    ap = {"x": x_ap, "ident": g["ident"],
                          "wgs": din("moe_wgs", [9, 2048, D]),
                          "wus": din("moe_wus", [9, 2048, D]),
                          "wds": din("moe_wds", [9, D, 2048]),
                          "rw": din("moe_rw", [8, D]),
                          "rb": din("moe_rb", [1, 8])}
                    moe_stage(nc, ch, st, ap, y_ap, pfx)
        with nc.Block() as block:
            ch.flush(block)
    return nc



CFGS = {
    "l0a": dict(name="l0a", causal=True, rope=True, decay=False, relbias=False,
                topk=False, gates=False, qscale=0.125),
    "l0b": dict(name="l0b", causal=True, rope=False, decay=False, relbias=False,
                topk=False, gates=False, qscale=0.125),
    "l1a": dict(name="l1a", causal=False, rope=False, decay=False, relbias=True,
                topk=True, gates=True, qscale=0.125),
    "l1b": dict(name="l1b", causal=True, rope=False, decay=True, relbias=False,
                topk=False, gates=False, qscale=0.125),
}

_consts = make_consts()
_nc_cache = {}


def _get_nc(kind):
    if kind not in _nc_cache:
        if kind == "ffn":
            _nc_cache[kind] = build_ffn()
        elif kind == "moe":
            _nc_cache[kind] = build_moe()
        else:
            _nc_cache[kind] = build_attn(CFGS[kind])
    return _nc_cache[kind]


def attn_core_inputs(kind, c, x, wq, wk, wv, bq, bk, bv, wo, bo,
                     head_gates=None, rel_pos_bias=None):
    cfg = CFGS[kind]
    b, half = c // 2, c % 2
    hs = slice(512 * half, 512 * half + 512)
    m = {
        "x": np.ascontiguousarray(x[b, ROWS * half:ROWS * (half + 1)]),
        "wqkv": np.ascontiguousarray(np.concatenate([wq[hs], wk[hs], wv[hs]], 0)),
        "bqkv": np.ascontiguousarray(np.concatenate([bq[hs], bk[hs], bv[hs]])[None, :]),
        "wo": np.ascontiguousarray(wo[:, 512 * half:512 * half + 512]),
        "bo": np.ascontiguousarray(bo[None, :]),
        "ident": _consts["ident"],
    }
    if cfg["causal"]:
        for s_ in range(4):
            m[f"triq_{s_}"] = _consts[f"triq_{s_}"]
            m[f"trik_{s_}"] = _consts[f"trik_{s_}"]
    if cfg["decay"]:
        m["poskq"] = _consts["poskq"]
        m["poskk"] = _consts["poskk"]
        for d_ in range(4):
            m[f"trikq_{d_}"] = _consts[f"trikq_{d_}"]
    if cfg["rope"]:
        m["ropec"] = _consts["ropec"]
        m["ropes"] = _consts["ropes"]
    if cfg["topk"]:
        m["identf"] = _consts["identf"]
    if cfg["relbias"]:
        heads = range(MYH * half, MYH * half + MYH)
        bexr = np.zeros((MYH, LQT), dtype=np.float32)
        bex = np.zeros((MYH, LQT), dtype=np.float32)
        bfar = np.zeros((1, 2 * MYH), dtype=np.float32)
        idx = np.clip(TMAX - np.arange(LQT), -511, 511) + 511
        idx2 = np.clip(np.arange(LQT) - 1279, -511, 511) + 511
        for j, h in enumerate(heads):
            bexr[j] = rel_pos_bias[idx, h]
            bex[j] = rel_pos_bias[idx2, h]
            bfar[0, 2 * j] = rel_pos_bias[0, h]
            bfar[0, 2 * j + 1] = rel_pos_bias[1022, h]
        m["bexr"] = bexr
        m["bex"] = bex
        m["bfar"] = bfar
    if cfg["gates"]:
        m["gates"] = np.ascontiguousarray(head_gates[MYH * half:MYH * half + MYH][None, :])
    return m


def _scatter(x, res):
    xn = np.empty_like(x)
    for c in range(NCORE):
        b, half = c // 2, c % 2
        xn[b, ROWS * half:ROWS * (half + 1)] = res.results[c]["xo"]
    return xn


def _run_attn(kind, x, mkmap):
    maps = [mkmap(c) for c in range(NCORE)]
    res = run_bass_kernel_spmd(_get_nc(kind), maps, core_ids=list(range(NCORE)))
    return _scatter(x, res)


def _run_rowlocal(kind, x, extra):
    maps = []
    for c in range(NCORE):
        b, half = c // 2, c % 2
        m = {"x": np.ascontiguousarray(x[b, ROWS * half:ROWS * (half + 1)]),
             "ident": _consts["ident"]}
        m.update(extra)
        maps.append(m)
    res = run_bass_kernel_spmd(_get_nc(kind), maps, core_ids=list(range(NCORE)))
    return _scatter(x, res)


def kernel(L0_attn_a_X, L0_attn_a_W_KQV, L0_attn_a_W_O, L0_attn_b_W_qkv,
           L0_attn_b_b_qkv, L0_attn_b_W_o, L0_attn_b_b_o, L0_ffn_weight,
           L1_attn_a_Wq, L1_attn_a_bq, L1_attn_a_Wk, L1_attn_a_bk, L1_attn_a_Wv,
           L1_attn_a_bv, L1_attn_a_Wo, L1_attn_a_bo, L1_attn_a_head_gates,
           L1_attn_a_rel_pos_bias, L1_attn_b_Wq, L1_attn_b_bq, L1_attn_b_Wk,
           L1_attn_b_bk, L1_attn_b_Wv, L1_attn_b_bv, L1_attn_b_Wo, L1_attn_b_bo,
           L1_ffn_sg_w, L1_ffn_su_w, L1_ffn_sd_w, L1_ffn_router_w,
           L1_ffn_routing_bias, L1_ffn_wg, L1_ffn_wu, L1_ffn_wd):
    f32 = np.float32
    x = np.asarray(L0_attn_a_X, dtype=f32)
    z1 = np.zeros(D, f32)
    kqv = np.asarray(L0_attn_a_W_KQV, f32)
    wqkv_b = np.asarray(L0_attn_b_W_qkv, f32)
    bqkv_b = np.asarray(L0_attn_b_b_qkv, f32)
    attn_args = {
        "a0_": ("l0a", kqv[D:2 * D], kqv[0:D], kqv[2 * D:3 * D], z1, z1, z1,
                np.asarray(L0_attn_a_W_O, f32), z1, None, None),
        "b0_": ("l0b", wqkv_b[0:D], wqkv_b[D:2 * D], wqkv_b[2 * D:], bqkv_b[0:D],
                bqkv_b[D:2 * D], bqkv_b[2 * D:], np.asarray(L0_attn_b_W_o, f32),
                np.asarray(L0_attn_b_b_o, f32), None, None),
        "c0_": ("l1a", np.asarray(L1_attn_a_Wq, f32), np.asarray(L1_attn_a_Wk, f32),
                np.asarray(L1_attn_a_Wv, f32), np.asarray(L1_attn_a_bq, f32),
                np.asarray(L1_attn_a_bk, f32), np.asarray(L1_attn_a_bv, f32),
                np.asarray(L1_attn_a_Wo, f32), np.asarray(L1_attn_a_bo, f32),
                np.asarray(L1_attn_a_head_gates, f32),
                np.asarray(L1_attn_a_rel_pos_bias, f32)),
        "d0_": ("l1b", np.asarray(L1_attn_b_Wq, f32), np.asarray(L1_attn_b_Wk, f32),
                np.asarray(L1_attn_b_Wv, f32), np.asarray(L1_attn_b_bq, f32),
                np.asarray(L1_attn_b_bk, f32), np.asarray(L1_attn_b_bv, f32),
                np.asarray(L1_attn_b_Wo, f32), np.asarray(L1_attn_b_bo, f32),
                None, None),
    }
    if "fused" not in _nc_cache:
        _nc_cache["fused"] = build_fused(CFGS)
    nc = _nc_cache["fused"]
    maps = []
    for c in range(NCORE):
        b, half = c // 2, c % 2
        m = {"x": np.ascontiguousarray(x[b, ROWS * half:ROWS * (half + 1)]),
             "ident": _consts["ident"], "identf": _consts["identf"],
             "poskq": _consts["poskq"], "poskk": _consts["poskk"],
             "ropec": _consts["ropec"], "ropes": _consts["ropes"]}
        for s_ in range(4):
            m[f"triq_{s_}"] = _consts[f"triq_{s_}"]
            m[f"trik_{s_}"] = _consts[f"trik_{s_}"]
            m[f"trikq_{s_}"] = _consts[f"trikq_{s_}"]
        for pfx, args in attn_args.items():
            kind, wq, wk, wv, bq, bk, bv, wo, bo, hg, rpb = args
            am = attn_core_inputs(kind, c, x, wq, wk, wv, bq, bk, bv, wo, bo,
                                  head_gates=hg, rel_pos_bias=rpb)
            for k in ("wqkv", "bqkv", "wo", "bo", "bexr", "bex", "bfar", "gates"):
                if k in am:
                    m[pfx + k] = am[k]
        m["ffn_w"] = np.ascontiguousarray(np.asarray(L0_ffn_weight, f32))
        m["moe_wgs"] = np.ascontiguousarray(np.concatenate(
            [np.asarray(L1_ffn_wg, f32), np.asarray(L1_ffn_sg_w, f32)[None]], 0))
        m["moe_wus"] = np.ascontiguousarray(np.concatenate(
            [np.asarray(L1_ffn_wu, f32), np.asarray(L1_ffn_su_w, f32)[None]], 0))
        m["moe_wds"] = np.ascontiguousarray(np.concatenate(
            [np.asarray(L1_ffn_wd, f32), np.asarray(L1_ffn_sd_w, f32)[None]], 0))
        m["moe_rw"] = np.ascontiguousarray(np.asarray(L1_ffn_router_w, f32))
        m["moe_rb"] = np.ascontiguousarray(np.asarray(L1_ffn_routing_bias, f32)[None, :])
        maps.append(m)
    res = run_bass_kernel_spmd(nc, maps, core_ids=list(range(NCORE)))
    xn = np.empty_like(x)
    for c in range(NCORE):
        b, half = c // 2, c % 2
        xn[b, ROWS * half:ROWS * (half + 1)] = res.results[c]["xo"]
    return xn
